# version 20
# baseline (speedup 1.0000x reference)
import numpy as np
import ml_dtypes
import concourse.bass as bass
import concourse.mybir as mybir
from concourse.bass_utils import run_bass_kernel_spmd
from contextlib import ExitStack

F32 = mybir.dt.float32
BF16 = mybir.dt.bfloat16
AF = mybir.ActivationFunctionType
ALU = mybir.AluOpType
AXX = mybir.AxisListType.X

P, C, DFF = 128, 1024, 4096
CB, FB = C // P, DFF // P
NG, G = 4, 2
NQ = 4
FQ = DFF // NQ


class _Sem:
    __slots__ = ("h", "n")
    def __init__(self, h):
        self.h = h
        self.n = 0


def build_program(T=4096, L=512, pipeline_A=True, s_rden=True):
    NT = T // L
    nc = bass.Bass("TRN2", target_bir_lowering=False,
                   detect_race_conditions=False)

    XF = nc.dram_tensor("XF", [P, T * CB], F32, kind="ExternalInput").ap()
    WKT = nc.dram_tensor("WKT", [P, CB * C], F32, kind="ExternalInput").ap()
    WVT = nc.dram_tensor("WVT", [P, CB * C], F32, kind="ExternalInput").ap()
    WRT = nc.dram_tensor("WRT", [P, CB * C], F32, kind="ExternalInput").ap()
    WOT = nc.dram_tensor("WOT", [P, CB * C], F32, kind="ExternalInput").ap()
    WUT = nc.dram_tensor("WUT", [P, CB * DFF], F32, kind="ExternalInput").ap()
    WDT = nc.dram_tensor("WDT", [P, CB * DFF], F32, kind="ExternalInput").ap()
    TD8 = nc.dram_tensor("TD8", [P, CB], F32, kind="ExternalInput").ap()
    TF8 = nc.dram_tensor("TF8", [P, CB], F32, kind="ExternalInput").ap()
    MK8 = nc.dram_tensor("MK8", [P, CB], F32, kind="ExternalInput").ap()
    MV8 = nc.dram_tensor("MV8", [P, CB], F32, kind="ExternalInput").ap()
    MR8 = nc.dram_tensor("MR8", [P, CB], F32, kind="ExternalInput").ap()
    OUT = nc.dram_tensor("OUT", [C, T], F32, kind="ExternalOutput").ap()
    UPB = nc.dram_tensor("UPB", [P, CB * DFF], BF16, kind="Internal").ap()
    DNB = nc.dram_tensor("DNB", [P, CB * DFF], BF16, kind="Internal").ap()
    WKB = nc.dram_tensor("WKB", [P, CB * C], BF16, kind="Internal").ap()
    WVB = nc.dram_tensor("WVB", [P, CB * C], BF16, kind="Internal").ap()
    WRB = nc.dram_tensor("WRB", [P, CB * C], BF16, kind="Internal").ap()
    WOB = nc.dram_tensor("WOB", [P, CB * C], BF16, kind="Internal").ap()

    XFP4 = XF.rearrange("p (n c l) -> p n c l", n=NT, c=CB)

    ctx = ExitStack()
    sb = lambda n, s, d=F32: ctx.enter_context(nc.sbuf_tensor(n, s, d)).ap()
    ps = lambda n, s: ctx.enter_context(nc.psum_tensor(n, s, F32)).ap()
    mksem = lambda n: _Sem(ctx.enter_context(nc.semaphore(n)))

    ones_f = sb("ones_f", [P, P])
    ones_b = sb("ones_b", [P, P], BF16)
    td8 = sb("td8", [P, CB]); tf8 = sb("tf8", [P, CB]); dcol = sb("dcol", [P, CB])
    mk8 = sb("mk8", [P, CB]); mv8 = sb("mv8", [P, CB]); mr8 = sb("mr8", [P, CB])
    omk8 = sb("omk8", [P, CB]); omv8 = sb("omv8", [P, CB]); omr8 = sb("omr8", [P, CB])
    sprev = sb("sprev", [P, CB]); snew = sb("snew", [P, CB])
    m8 = sb("m8", [P, CB]); d8 = sb("d8", [P, CB]); ns8 = sb("ns8", [P, CB])
    tfe8 = sb("tfe8", [P, CB]); resc8 = sb("resc8", [P, CB])
    carA = sb("carA", [P, CB]); carAv = sb("carAv", [P, CB])
    a0c = sb("a0c", [P, CB]); av0c = sb("av0c", [P, CB])
    xprev = sb("xprev", [P, CB], BF16)
    thrU = sb("thrU", [P, 1]); nthrU = sb("nthrU", [P, 1])
    ln05 = sb("ln05", [P, 1])
    thrD = sb("thrD", [P, 1]); nthrD = sb("nthrD", [P, 1])
    CEL = 2 * L
    NCH = C * DFF // (P * CEL)
    accW = sb("accW", [P, 2 * NCH]); accT = sb("accT", [P, 1])

    x_sb = sb("x_sb", [P, CB * L])
    xn = sb("xn", [P, CB * L], BF16)
    xmA = sb("xmA", [P, CB * L], BF16)
    xmB = sb("xmB", [P, CB * L], BF16)
    v_bf = sb("v_bf", [P, CB * L], BF16)
    sr = sb("sr", [P, CB * L], BF16)
    rwkv = sb("rwkv", [P, CB * L], BF16)
    h = sb("h", [P, FB * L], BF16)
    pu4 = [sb(f"pu{i}", [P, G * L]) for i in range(NG)]
    pwb = sb("pwb", [P, G * L])
    pAb = sb("pAb", [P, G * L])
    pVb = sb("pVb", [P, G * L])
    pnum = sb("pnum", [P, G * L])
    mean = sb("mean", [P, L])
    rxn05 = sb("rxn05", [P, L]); rh05 = sb("rh05", [P, L])
    xsq = [sb(f"xsq{i}", [P, L]) for i in range(2)]
    out_sb = [sb(f"outsb{i}", [P, L]) for i in range(4)]
    xres = [sb(f"xres{i}", [P, L]) for i in range(4)]
    hsq = [sb(f"hsq{i}", [P, L], BF16) for i in range(2)]
    slots = [sb(f"slot{i}", [P, CB * 1024], BF16) for i in range(3)]

    pb_full = [ps(f"pb{i}", [P, 512]) for i in range(8)]
    pb = [t[:, 0:L] for t in pb_full]

    x3 = x_sb.rearrange("p (cb l) -> p cb l", cb=CB)
    xn3 = xn.rearrange("p (cb l) -> p cb l", cb=CB)
    xmA3 = xmA.rearrange("p (cb l) -> p cb l", cb=CB)
    xmB3 = xmB.rearrange("p (cb l) -> p cb l", cb=CB)

    mean_b = mean.unsqueeze(1).broadcast_to((P, CB, L))
    rstd_b = xsq[1].unsqueeze(1).broadcast_to((P, CB, L))
    rxn_b = rxn05.unsqueeze(1).broadcast_to((P, CB, L))

    s_ld = mksem("s_ld")
    s_x = mksem("s_x")
    s_xr = [mksem(f"s_xr{i}") for i in range(4)]
    s_o = [mksem(f"s_o{i}") for i in range(4)]
    s_w = [mksem(f"s_w{i}") for i in range(3)]
    s_p = [mksem(f"s_p{i}") for i in range(8)]
    s_pw = mksem("s_pw")
    st = mksem("st")
    sv = mksem("sv")
    ss = mksem("ss")
    sg = mksem("sg")

    ops = {e: [] for e in ("sync", "gpsimd", "vector", "scalar", "tensor")}

    def op(eng, fn, waits=(), incs=()):
        rec = []
        for s, k in incs:
            s.n += k
            rec.append((s, k))
        ops[eng].append(([(s, v) for s, v in waits if v > 0], rec, fn))

    def W(s):
        return (s, s.n)

    def WV(s, v):
        return (s, v)

    op("sync", lambda e: e.dma_start(out=td8, in_=TD8), incs=[(s_ld, 16)])
    op("sync", lambda e: e.dma_start(out=tf8, in_=TF8), incs=[(s_ld, 16)])
    op("sync", lambda e: e.dma_start(out=mk8, in_=MK8), incs=[(s_ld, 16)])
    op("sync", lambda e: e.dma_start(out=mv8, in_=MV8), incs=[(s_ld, 16)])
    op("sync", lambda e: e.dma_start(out=mr8, in_=MR8), incs=[(s_ld, 16)])

    op("vector", lambda e: e.memset(ones_f, 1.0))
    op("vector", lambda e: e.memset(ones_b, 1.0))
    op("vector", lambda e: e.memset(sprev, -1.0e30))
    op("vector", lambda e: e.memset(carA, 0.0))
    op("vector", lambda e: e.memset(carAv, 0.0))
    op("vector", lambda e: e.memset(ln05, -0.6931471805599453))
    op("vector", lambda e: e.memset(xprev, 0.0), incs=[(sv, 1)])
    sv_setup = sv.n
    op("vector", lambda e: e.tensor_scalar(out=omk8, in0=mk8, scalar1=-1.0,
                                           scalar2=1.0, op0=ALU.mult, op1=ALU.add),
       waits=[W(s_ld)])
    op("vector", lambda e: e.tensor_scalar(out=omv8, in0=mv8, scalar1=-1.0,
                                           scalar2=1.0, op0=ALU.mult, op1=ALU.add))
    op("vector", lambda e: e.tensor_scalar(out=omr8, in0=mr8, scalar1=-1.0,
                                           scalar2=1.0, op0=ALU.mult, op1=ALU.add),
       incs=[(sv, 1)])
    sv_omix = sv.n
    op("scalar", lambda e: e.activation(out=tfe8, in_=tf8, func=AF.Exp),
       waits=[W(s_ld)], incs=[(ss, 1)])
    op("scalar", lambda e: e.activation(out=dcol, in_=td8, func=AF.Exp),
       incs=[(ss, 1)])
    op("scalar", lambda e: e.activation(out=dcol, in_=dcol, func=AF.Exp, scale=-1.0),
       waits=[W(ss)], incs=[(ss, 1)])
    ss_dcol = ss.n

    slot_free_at = [0, 0, 0]
    slot_pw_gate = [0, 0, 0]

    def load_slot(sl, fn_builder, extra_waits=()):
        waits = [WV(st, slot_free_at[sl])] + list(extra_waits)
        if slot_pw_gate[sl]:
            waits.append(WV(s_pw, slot_pw_gate[sl]))
            slot_pw_gate[sl] = 0
        op("gpsimd", fn_builder(slots[sl]), waits=waits, incs=[(s_w[sl], 16)])
        return sl, s_w[sl].n

    def attn_w_loader(src):
        def b(slot):
            return lambda e: e.dma_start(out=slot, in_=src)
        return b

    def load_attn(sl_pref, wf32, wbf, key, extra_waits=()):
        if key not in attn_bf:
            sl, cnt = load_slot(sl_pref, attn_w_loader(wf32),
                                extra_waits=extra_waits)
            op("gpsimd", lambda e, sl=sl, wbf=wbf: e.dma_start(
                out=wbf, in_=slots[sl]), incs=[(s_pw, 16)])
            attn_bf[key] = s_pw.n
            slot_pw_gate[sl] = s_pw.n
            return sl, cnt
        ex = [WV(s_pw, attn_bf[key])] + list(extra_waits)
        return load_slot(sl_pref, attn_w_loader(wbf), extra_waits=ex)

    attn_bf = {}

    def up_loader(q):
        def b(slot):
            return lambda e: e.dma_start(
                out=slot, in_=UPB[:, q * CB * FQ:(q + 1) * CB * FQ])
        return b

    def dn_loader(q):
        def b(slot):
            return lambda e: e.dma_start(
                out=slot, in_=DNB[:, q * CB * C:(q + 1) * CB * C])
        return b

    prep = {}

    h_f32 = h.bitcast(F32)
    ubufs = [h_f32[:, i * CEL:(i + 1) * CEL] for i in range(8)] + pu4
    ubuf_t = [h[:, 2 * i * CEL:2 * i * CEL + CEL] for i in range(8)] + \
             [t.bitcast(BF16)[:, 0:CEL] for t in pu4]
    xmA_f = xmA.bitcast(F32)
    vbf_f = v_bf.bitcast(F32)
    dbufs = [pwb, pAb, pVb, pnum,
             xmA_f[:, 0:CEL], xmA_f[:, CEL:2 * CEL],
             vbf_f[:, 0:CEL], vbf_f[:, CEL:2 * CEL]]
    dbuf_t = [t.bitcast(BF16)[:, 0:CEL] for t in (pwb, pAb, pVb, pnum)] + \
             [xmA[:, 0:CEL], xmA[:, 2 * CEL:3 * CEL],
              v_bf[:, 0:CEL], v_bf[:, 2 * CEL:3 * CEL]]
    scr_u = [sr[:, i * CEL:(i + 1) * CEL] for i in range(4)]
    scr_d = [rwkv[:, i * CEL:(i + 1) * CEL] for i in range(4)]

    def _pp(which):
        if which == "up":
            return WUT, UPB, ubufs, scr_u, thrU, nthrU, 0, ubuf_t
        return WDT, DNB, dbufs, scr_d, thrD, nthrD, NCH, dbuf_t

    def _prep_load(which, ci, ph):
        wsrc, _, bufs, _, _, _, _, _ = _pp(which)
        if ci >= NCH:
            return
        b = ci % len(bufs)
        si = (0 if which == "up" else 4) + ci % 4
        waits = []
        lu = prep.get((which, "lastuse", b), None)
        if lu is not None:
            waits.append(lu)
        if ph == 0:
            if ci >= 4:
                waits.append(WV(ss, prep[(which, "ause", ci - 4)]))
        else:
            waits.append(WV(ss, prep[(which, "tuse", ci - 4)] if ci >= 4
                            else prep[(which, "ause", NCH - 4 + ci)]))
        op("gpsimd", lambda e, ci=ci, b=b, bufs=bufs, wsrc=wsrc:
           e.dma_start(out=bufs[b], in_=wsrc[:, ci * CEL:(ci + 1) * CEL]),
           waits=waits, incs=[(s_p[si], 16)])
        prep[(which, "ld", ci)] = (si, s_p[si].n)

    def emit_prep_alpha_chunk(which, ci):
        _, _, bufs, _, _, _, acol, _ = _pp(which)
        nb = len(bufs)
        if ci == 0:
            for c2 in range(min(4, NCH)):
                _prep_load(which, c2, 0)
        b = ci % nb
        si, cnt = prep[(which, "ld", ci)]
        op("scalar", lambda e, b=b, ci=ci, bufs=bufs, acol=acol:
           e.activation(out=bufs[b], in_=bufs[b], func=AF.Abs,
                        accum_out=accW[:, acol + ci:acol + ci + 1]),
           waits=[WV(s_p[si], cnt)], incs=[(ss, 1)])
        prep[(which, "ause", ci)] = ss.n
        prep[(which, "lastuse", b)] = WV(ss, ss.n)
        _prep_load(which, ci + 4, 0)

    def emit_prep_alpha_fin(which):
        acol = _pp(which)[6]
        op("vector", lambda e, acol=acol:
           e.tensor_reduce(out=accT, in_=accW[:, acol:acol + NCH],
                           axis=AXX, op=ALU.add),
           waits=[W(ss)], incs=[(sv, 1)])
        sv_acc = sv.n
        op("tensor", lambda e: e.matmul(out=pb[5][:, 0:1], lhsT=ones_f,
                                        rhs=accT, start=True, stop=True),
           waits=[WV(sv, sv_acc), WV(sv, sv_setup),
                  WV(sv, bank_free[5][0]), WV(ss, bank_free[5][1])],
           incs=[(st, 1)])
        st_mm = st.n
        thr = thrU if which == "up" else thrD
        nthr = nthrU if which == "up" else nthrD
        sc = 0.5 / float(C * DFF)
        op("scalar", lambda e, thr=thr, sc=sc:
           e.mul(out=thr, in_=pb[5][:, 0:1], mul=sc),
           waits=[WV(st, st_mm)], incs=[(ss, 1)])
        op("scalar", lambda e, nthr=nthr, sc=sc:
           e.mul(out=nthr, in_=pb[5][:, 0:1], mul=-sc), incs=[(ss, 1)])
        prep[(which, "thr")] = ss.n
        bank_free[5] = (0, ss.n)

    def emit_prep_tern_chunk(which, ci, oi):
        _, wdst, bufs, scr, thr, nthr, _, tbufs = _pp(which)
        nb = len(bufs)
        if oi == 0:
            for c2 in range(min(4, NCH)):
                _prep_load(which, c2, 1)
        b = ci % nb
        s1 = scr[2 * (oi % 2)]
        s2 = scr[2 * (oi % 2) + 1]
        si, cnt = prep[(which, "ld", ci)]
        wts = [WV(ss, prep[(which, "thr")]),
               WV(s_p[si], cnt),
               WV(sv, prep.get((which, "tadd", oi - 2), 0))]
        op("scalar", lambda e, b=b, s1=s1, thr=thr, bufs=bufs:
           e.sign(out=s1, in_=bufs[b], bias=thr), waits=wts)
        op("scalar", lambda e, b=b, s2=s2, nthr=nthr, bufs=bufs:
           e.sign(out=s2, in_=bufs[b], bias=nthr), incs=[(ss, 1)])
        prep[(which, "tuse", ci)] = ss.n
        op("vector", lambda e, b=b, s1=s1, s2=s2, tbufs=tbufs:
           e.tensor_add(out=tbufs[b], in0=s1, in1=s2),
           waits=[W(ss)], incs=[(sv, 1)])
        prep[(which, "tadd", oi)] = sv.n
        op("gpsimd", lambda e, ci=ci, b=b, wdst=wdst, tbufs=tbufs:
           e.dma_start(out=wdst[:, ci * CEL:(ci + 1) * CEL], in_=tbufs[b]),
           waits=[WV(sv, sv.n)], incs=[(s_pw, 16)])
        prep[(which, "lastuse", b)] = WV(s_pw, s_pw.n)
        _prep_load(which, ci + 4, 1)

    def _done(which):
        prep[(which, "wdone")] = s_pw.n

    prep_q = []
    for ci in range(NCH):
        prep_q.append(lambda ci=ci: emit_prep_alpha_chunk("up", ci))
        prep_q.append(lambda ci=ci: emit_prep_alpha_chunk("dn", ci))
    prep_q.append(lambda: emit_prep_alpha_fin("up"))
    prep_q.append(lambda: emit_prep_alpha_fin("dn"))
    prep_q += [lambda ci=ci: emit_prep_tern_chunk("up", ci, ci)
               for ci in range(NCH)]
    prep_q.append(lambda: _done("up"))
    prep_q += [lambda ci=ci: emit_prep_tern_chunk("dn", ci, ci)
               for ci in range(NCH)]
    prep_q.append(lambda: _done("dn"))
    dn_q = []

    def drain(q, n):
        for _ in range(min(n, len(q))):
            q.pop(0)()

    bank_free = [(0, 0) for _ in range(8)]
    state = {"wkv_sv": {}, "hsq_st": {}, "xsq_st": {}, "oso": [0, 0, 0, 0],
             "xA": None, "phA": {}, "st_up": 0,
             "st_r": 0, "st_v": 0, "st_k": 0, "xrso": [0, 0, 0, 0]}

    def emit_phaseA1(ti):
        t0 = ti * L
        xw = []
        if state["xA"] is not None:
            a = state["xA"]
            xw = [WV(st, a["st"]), WV(ss, a["ss"]), WV(sv, a["sv"]),
                  WV(s_pw, state.pop("xA_pw", 0))]
        op("sync", lambda e, ti=ti: e.dma_start(out=x3, in_=XFP4[:, ti]),
           waits=xw, incs=[(s_x, 16)])
        sx_here = s_x.n
        for cb in range(CB):
            wts = []
            if cb == 0:
                wts = [WV(s_x, sx_here), WV(sv, sv_setup),
                       WV(sv, bank_free[7][0]), WV(ss, bank_free[7][1])]
            op("tensor", lambda e, cb=cb: e.matmul(
                out=pb[7], lhsT=ones_f, rhs=x3[:, cb, :],
                start=(cb == 0), stop=(cb == CB - 1)),
               waits=wts, incs=([(st, 1)] if cb == CB - 1 else []))
        st_sx = st.n
        op("vector", lambda e: e.tensor_scalar_mul(out=mean, in0=pb[7],
                                                   scalar1=1.0 / C),
           waits=[WV(st, st_sx)], incs=[(sv, 1)])
        sv_mean = sv.n
        for cb in range(CB):
            par = cb % 2
            op("scalar", lambda e, cb=cb, par=par: e.activation(
                out=xsq[par], in_=x3[:, cb, :], func=AF.Square),
               waits=[WV(s_x, sx_here), WV(st, state["xsq_st"].get(par, 0))],
               incs=[(ss, 1)])
            ss_xsq = ss.n
            wts = [WV(ss, ss_xsq)]
            if cb == 0:
                wts += [WV(sv, sv_mean)]
            op("tensor", lambda e, par=par, cb=cb: e.matmul(
                out=pb[7], lhsT=ones_f, rhs=xsq[par],
                start=(cb == 0), stop=(cb == CB - 1)),
               waits=wts, incs=[(st, 1)])
            state["xsq_st"][par] = st.n
        st_sx2 = st.n
        ss_sq = ss.n
        op("vector", lambda e: e.tensor_scalar_mul(out=xsq[0], in0=pb[7],
                                                   scalar1=1.0 / C),
           waits=[WV(st, st_sx2)], incs=[(sv, 1)])
        sv_b67 = sv.n
        bank_free[7] = (sv_b67, 0)
        return dict(sx=sx_here, sv_b67=sv_b67, st_sx2=st_sx2, ss_sq=ss_sq)

    def emit_phaseA2a():
        op("vector", lambda e: e.tensor_mul(out=xsq[1], in0=mean, in1=mean))
        op("vector", lambda e: e.tensor_sub(out=xsq[0], in0=xsq[0], in1=xsq[1]))
        op("vector", lambda e: e.tensor_scalar_add(out=xsq[1], in0=xsq[0],
                                                   scalar1=1.0e-5),
           incs=[(sv, 1)])
        sv_vpe = sv.n
        op("scalar", lambda e: e.activation(out=xsq[1], in_=xsq[1], func=AF.Ln),
           waits=[WV(sv, sv_vpe)], incs=[(ss, 1)])
        op("scalar", lambda e: e.activation(out=rh05, in_=xsq[0], func=AF.Ln),
           waits=[WV(sv, state.get("sv_hn", 0))],
           incs=[(ss, 1)])
        ss_ln = ss.n
        op("vector", lambda e: e.tensor_sub(out=rh05, in0=rh05, in1=xsq[1]),
           waits=[WV(ss, ss_ln)], incs=[(sv, 1)])
        op("scalar", lambda e: e.activation(out=rh05, in_=rh05, func=AF.Exp,
                                            scale=0.5), waits=[W(sv)],
           incs=[(ss, 1)])
        op("scalar", lambda e: e.activation(out=xsq[1], in_=xsq[1], func=AF.Exp,
                                            scale=-0.5), incs=[(ss, 1)])
        ss_e = ss.n
        op("vector", lambda e: e.tensor_scalar_add(out=rh05, in0=rh05,
                                                   scalar1=1.0e-8),
           waits=[WV(ss, ss_e)], incs=[(sv, 1)])
        op("scalar", lambda e: e.activation(out=rh05, in_=rh05, func=AF.Ln),
           waits=[W(sv)], incs=[(ss, 1)])
        op("scalar", lambda e: e.activation(out=rxn05, in_=rh05, func=AF.Exp,
                                            scale=-1.0, bias=ln05),
           incs=[(ss, 1)])
        return ss_e, ss.n

    def emit_phaseA2b(ti, a1, ss_e, ss_rxn):
        xnn_cb = []
        for cb in range(0, CB, 2):
            wts = ([WV(st, state["st_v"]), WV(ss, ss_e)] if cb == 0 else [])
            op("vector", lambda e, cb=cb: e.tensor_sub(
                out=xmB3[:, cb:cb + 2, :], in0=x3[:, cb:cb + 2, :],
                in1=mean_b[:, cb:cb + 2, :]), waits=wts)
            op("vector", lambda e, cb=cb: e.tensor_tensor(
                out=xn3[:, cb:cb + 2, :], in0=xmB3[:, cb:cb + 2, :],
                in1=rstd_b[:, cb:cb + 2, :], op=ALU.mult))
            op("vector", lambda e, cb=cb: e.tensor_tensor(
                out=xmB3[:, cb:cb + 2, :], in0=xn3[:, cb:cb + 2, :],
                in1=rxn_b[:, cb:cb + 2, :], op=ALU.mult),
               waits=([WV(ss, ss_rxn)] if cb == 0 else []), incs=[(sv, 1)])
            xnn_cb += [sv.n, sv.n]
        sv_xnn = sv.n
        state["phA"][ti] = dict(sv_xnn=sv_xnn, xnn_cb=xnn_cb, **a1)
        state["xA"] = dict(st=a1["st_sx2"], ss=a1["ss_sq"], sv=sv_xnn)

    def emit_lerp(mix, omix, dst3, extra_w=()):
        for cb in range(CB):
            wts = list(extra_w) if cb == 0 else []
            op("vector", lambda e, cb=cb, mix=mix, dst3=dst3:
               e.tensor_scalar_mul(out=dst3[:, cb, :], in0=xn3[:, cb, :],
                                   scalar1=mix[:, cb:cb + 1]), waits=wts)
            op("vector", lambda e, cb=cb, omix=omix, dst3=dst3:
               e.scalar_tensor_tensor(
                   out=dst3[:, cb, 1:L], in0=xn3[:, cb, 0:L - 1],
                   scalar=omix[:, cb:cb + 1], in1=dst3[:, cb, 1:L],
                   op0=ALU.mult, op1=ALU.add))
        op("vector", lambda e, omix=omix: e.tensor_tensor(
            out=d8, in0=xprev, in1=omix, op=ALU.mult),
           waits=[WV(ss, state.get("resc_ss", 0))], incs=[(sv, 1)])
        op("vector", lambda e, dst3=dst3: e.tensor_tensor(
            out=dst3[:, :, 0], in0=dst3[:, :, 0], in1=d8, op=ALU.add),
           waits=[W(sv)], incs=[(sv, 1)])
        return sv.n

    for ti in range(NT):
        t0 = ti * L
        if ti == 0 or not pipeline_A:
            a1 = emit_phaseA1(ti)
            a2i = emit_phaseA2a()
            emit_phaseA2b(ti, a1, *a2i)
        phA = state["phA"][ti]

        if ti == 0:
            drain(prep_q, len(prep_q))
        sv_lr = emit_lerp(mr8, omr8, xmA3, extra_w=[
            WV(st, state["st_k"]),
            WV(s_pw, prep.get(("dn", "wdone"), 0))])

        up_extra = ([WV(s_pw, prep.get(("up", "wdone"), 0))] if ti == 0 else [])
        upu = [None] * NQ
        upu[0] = load_slot(0, up_loader(0), extra_waits=up_extra)
        upu[1] = load_slot(1, up_loader(1), extra_waits=up_extra)
        upu[2] = load_slot(2, up_loader(2), extra_waits=up_extra)

        ubanks = [4, 5, 2, 3]
        for q in range(NQ):
            if q == 1:
                upu[3] = load_slot(0, up_loader(3), extra_waits=up_extra)
            if q == 2:
                ru = load_attn(1, WRT, WRB, "r")
            if q == 3:
                vu = load_attn(2, WVT, WVB, "v")
            usl, uw = upu[q]

            def emit_sh2(g):
                if g < 0:
                    return
                wts = [WV(ss, state[("sh2ss", g)])]
                if g == 0:
                    wts += [WV(sv, bank_free[6][0]), WV(ss, bank_free[6][1]),
                            WV(sv, sv_setup)]
                op("tensor", lambda e, g=g: e.matmul(
                    out=pb[6], lhsT=ones_b, rhs=hsq[g % 2],
                    start=(g == 0), stop=(g == FB - 1)),
                   waits=wts, incs=[(st, 1)])
                state["hsq_st"][g % 2] = st.n

            for fi in range(FQ // P):
                gblk = q * (FQ // P) + fi
                ub = ubanks[gblk % 4]
                bank = pb[ub]
                for cb in range(CB):
                    wts = []
                    if gblk == 0:
                        wts = [WV(sv, phA["xnn_cb"][cb])]
                    if cb == 0:
                        wts += [WV(s_w[usl], uw),
                                WV(sv, bank_free[ub][0]),
                                WV(ss, bank_free[ub][1])]
                    op("tensor", lambda e, cb=cb, fi=fi, bank=bank, usl=usl:
                       e.matmul(
                           out=bank,
                           lhsT=slots[usl][:, cb * FQ + fi * P:cb * FQ + (fi + 1) * P],
                           rhs=xmB3[:, cb, :], start=(cb == 0), stop=(cb == CB - 1)),
                       waits=wts, incs=([(st, 1)] if cb == CB - 1 else []))
                st_up = st.n
                emit_sh2(gblk - 2)
                op("scalar", lambda e, gblk=gblk, bank=bank: e.activation(
                    out=h[:, gblk * L:(gblk + 1) * L], in_=bank, func=AF.Gelu),
                   waits=[WV(st, st_up)], incs=[(ss, 1)])
                op("scalar", lambda e, gblk=gblk: e.activation(
                    out=hsq[gblk % 2], in_=h[:, gblk * L:(gblk + 1) * L],
                    func=AF.Square),
                   waits=[WV(st, state["hsq_st"].get(gblk % 2, 0))],
                   incs=[(ss, 1)])
                state[("sh2ss", gblk)] = ss.n
                bank_free[ub] = (0, ss.n)
            slot_free_at[usl] = st.n
        emit_sh2(FB - 2)
        emit_sh2(FB - 1)
        ku = load_attn(0, WKT, WKB, "k")
        st_sh2 = st.n
        state["st_up"] = st_sh2

        op("vector", lambda e: e.tensor_scalar_mul(out=rh05, in0=pb[6],
                                                   scalar1=1.0 / DFF),
           waits=[WV(st, st_sh2)], incs=[(sv, 1)])
        bank_free[6] = (sv.n, 0)
        op("scalar", lambda e: e.activation(out=rh05, in_=rh05, func=AF.Ln),
           waits=[W(sv)], incs=[(ss, 1)])
        op("scalar", lambda e: e.activation(out=rh05, in_=rh05, func=AF.Exp,
                                            scale=0.5), incs=[(ss, 1)])
        op("vector", lambda e: e.tensor_scalar_add(out=rh05, in0=rh05,
                                                   scalar1=1.0e-8),
           waits=[W(ss)], incs=[(sv, 1)])
        op("scalar", lambda e: e.activation(out=rh05, in_=rh05, func=AF.Ln),
           waits=[W(sv)], incs=[(ss, 1)])
        op("scalar", lambda e: e.activation(out=rh05, in_=rh05, func=AF.Exp,
                                            scale=-1.0, bias=ln05),
           incs=[(ss, 1)])
        ss_rh = ss.n
        sv_hn_q = [0] * NQ

        def emit_hn(qq):
            for gi in range(FB // NQ):
                gblk = qq * (FB // NQ) + gi
                wts = [WV(ss, ss_rh)] if gblk == 0 else []
                op("vector", lambda e, gblk=gblk: e.tensor_tensor(
                    out=h[:, gblk * L:(gblk + 1) * L],
                    in0=h[:, gblk * L:(gblk + 1) * L], in1=rh05, op=ALU.mult),
                   waits=wts,
                   incs=([(sv, 1)] if gi == FB // NQ - 1 else []))
            sv_hn_q[qq] = sv.n
            if qq == NQ - 1:
                state["sv_hn"] = sv.n

        def proj(slw, src3, sv_src, evac):
            sl, wld = slw
            for j in range(CB):
                bank = pb[j % 2]
                for cb in range(CB):
                    wts = []
                    if cb == 0:
                        wts = [WV(sv, sv_src), WV(s_w[sl], wld),
                               WV(sv, bank_free[j % 2][0]),
                               WV(ss, bank_free[j % 2][1])]
                    op("tensor", lambda e, j=j, cb=cb, bank=bank, sl=sl: e.matmul(
                        out=bank,
                        lhsT=slots[sl][:, cb * 1024 + j * P:cb * 1024 + (j + 1) * P],
                        rhs=src3[:, cb, :], start=(cb == 0), stop=(cb == CB - 1)),
                       waits=wts, incs=([(st, 1)] if cb == CB - 1 else []))
                evac(j, bank, st.n)
            slot_free_at[sl] = st.n
            return st.n

        def evac_r(j, bank, st_c):
            pw = ([WV(s_pw, prep.get(("up", "wdone"), 0))] if j == 0 else [])
            op("scalar", lambda e, j=j, bank=bank: e.activation(
                out=sr[:, j * L:(j + 1) * L], in_=bank, func=AF.Sigmoid),
               waits=[WV(st, st_c)] + pw, incs=[(ss, 1)])
            bank_free[j % 2] = (0, ss.n)

        def evac_v(j, bank, st_c):
            pw = ([WV(s_pw, prep.get(("dn", "wdone"), 0))] if j == 0 else [])
            op("scalar", lambda e, j=j, bank=bank: e.activation(
                out=v_bf[:, j * L:(j + 1) * L], in_=bank, func=AF.Copy),
               waits=[WV(st, st_c)] + pw, incs=[(ss, 1)])
            bank_free[j % 2] = (0, ss.n)

        state["st_r"] = proj(ru, xmA3, sv_lr, evac_r)
        dnu = [None] * NQ

        def dn_extra():
            return ([WV(s_pw, prep.get(("dn", "wdone"), 0))] if ti == 0 else [])
        if ti > 0:
            dnu[0] = load_slot(1, dn_loader(0))

        sv_lv = emit_lerp(mv8, omv8, xmB3, extra_w=[WV(st, state["st_up"])])
        state["st_v"] = proj(vu, xmB3, sv_lv, evac_v)
        if ti > 0:
            dnu[1] = load_slot(2, dn_loader(1))

        sv_lk = emit_lerp(mk8, omk8, xmA3, extra_w=[WV(st, state["st_r"])])
        op("vector", lambda e: e.tensor_copy(out=xprev, in_=xn3[:, :, L - 1]),
           incs=[(sv, 1)])
        emit_hn(0)
        ksl, kw = ku

        def k_mm_emit(g):
            kb = (g % 2) * 2
            for s in range(G):
                j = g * G + s
                bank = pb[kb + s]
                for cb in range(CB):
                    wts = []
                    if cb == 0:
                        wts = [WV(sv, sv_lk), WV(s_w[ksl], kw),
                               WV(sv, bank_free[kb + s][0]),
                               WV(ss, bank_free[kb + s][1])]
                    op("tensor", lambda e, j=j, cb=cb, bank=bank, ksl=ksl: e.matmul(
                        out=bank,
                        lhsT=slots[ksl][:, cb * 1024 + j * P:cb * 1024 + (j + 1) * P],
                        rhs=xmA3[:, cb, :], start=(cb == 0), stop=(cb == CB - 1)),
                       waits=wts, incs=([(st, 1)] if cb == CB - 1 else []))
                state[("st_kj", g, s)] = st.n
            state["st_k"] = st.n
            if g == NG - 1:
                slot_free_at[ksl] = st.n

        k_stats = []
        k_pools = []
        for g in range(NG):
            kb = (g % 2) * 2

            def stats_thunk(g=g, kb=kb, ti=ti):
                stk = (state[("st_kj", g, 0)], state[("st_kj", g, 1)])
                pu = pu4[g]
                c0, c1 = g * G, g * G + G
                op("vector", lambda e, c0=c0, kb=kb: e.tensor_reduce(
                    out=m8[:, c0:c0 + 1], in_=pb[kb], axis=AXX, op=ALU.max),
                   waits=[WV(st, stk[0])])
                op("vector", lambda e, c0=c0, kb=kb: e.tensor_reduce(
                    out=m8[:, c0 + 1:c0 + 2], in_=pb[kb + 1], axis=AXX,
                    op=ALU.max),
                   waits=[WV(st, stk[1])], incs=[(sv, 1)])
                op("vector", lambda e, c0=c0, c1=c1: e.tensor_max(
                    out=snew[:, c0:c1], in0=m8[:, c0:c1], in1=sprev[:, c0:c1]),
                   waits=[W(sv)], incs=[(sv, 1)])
                op("vector", lambda e, c0=c0, c1=c1: e.tensor_sub(
                    out=d8[:, c0:c1], in0=sprev[:, c0:c1], in1=snew[:, c0:c1]),
                   waits=[W(sv)])
                op("vector", lambda e, c0=c0, c1=c1: e.tensor_scalar_mul(
                    out=ns8[:, c0:c1], in0=snew[:, c0:c1], scalar1=-1.0))
                op("vector", lambda e, c0=c0, c1=c1: e.tensor_copy(
                    out=sprev[:, c0:c1], in_=snew[:, c0:c1]), incs=[(sv, 1)])
                sv_stats = sv.n
                pg = []
                if ti == 0:
                    pg = [WV(s_pw, max(prep.get(("up", "wdone"), 0),
                                       prep.get(("dn", "wdone"), 0)))]
                op("scalar", lambda e, c0=c0, c1=c1: e.activation(
                    out=resc8[:, c0:c1], in_=d8[:, c0:c1], func=AF.Exp),
                   waits=[WV(sv, sv_stats),
                          WV(sv, state.get(("pu_free", g), 0))] + pg,
                   incs=[(ss, 1)])
                state["resc_ss"] = ss.n
                for s in range(G):
                    j = g * G + s
                    bank = pb[kb + s]
                    op("scalar", lambda e, s=s, j=j, bank=bank, pu=pu:
                       e.activation(out=pu[:, s * L:(s + 1) * L], in_=bank,
                                    func=AF.Exp, bias=ns8[:, j:j + 1]),
                       incs=[(ss, 1)])
                    bank_free[kb + s] = (0, ss.n)
                state[("ss_exp", g)] = ss.n
            k_stats.append(stats_thunk)

            def pools_thunk(g=g, kb=kb, ti=ti, mid=None):
                pu = pu4[g]
                c0, c1 = g * G, g * G + G
                pA3 = pAb.rearrange("p (s l) -> p s l", s=G)
                pV3 = pVb.rearrange("p (s l) -> p s l", s=G)
                vsl = v_bf[:, c0 * L:c1 * L]
                srsl = sr[:, c0 * L:c1 * L]
                rden_prev = state.get("rden_ss", 0)
                op("vector", lambda e, c0=c0, c1=c1: e.tensor_tensor(
                    out=a0c[:, c0:c1], in0=carA[:, c0:c1], in1=resc8[:, c0:c1],
                    op=ALU.mult), waits=[WV(ss, state[("ss_exp", g)])])
                op("vector", lambda e, c0=c0, c1=c1: e.tensor_tensor(
                    out=av0c[:, c0:c1], in0=carAv[:, c0:c1],
                    in1=resc8[:, c0:c1], op=ALU.mult), incs=[(sv, 1)])
                sv_a0 = sv.n
                op("vector", lambda e, pu=pu, vsl=vsl: e.tensor_tensor(
                    out=pwb, in0=pu, in1=vsl, op=ALU.mult),
                   waits=[WV(ss, rden_prev)])
                for s in range(G):
                    j = g * G + s
                    op("vector", lambda e, s=s, j=j, pu=pu:
                       e.tensor_tensor_scan(
                           out=pAb[:, s * L:(s + 1) * L],
                           data0=dcol[:, j:j + 1].broadcast_to((P, L)),
                           data1=pu[:, s * L:(s + 1) * L],
                           initial=a0c[:, j:j + 1], op0=ALU.mult, op1=ALU.add),
                       waits=[WV(sv, sv_a0)] + (
                           [WV(ss, ss_dcol)] if (ti == 0 and g == 0 and s == 0)
                           else []))
                for s in range(G):
                    j = g * G + s
                    op("vector", lambda e, s=s, j=j:
                       e.tensor_tensor_scan(
                           out=pVb[:, s * L:(s + 1) * L],
                           data0=dcol[:, j:j + 1].broadcast_to((P, L)),
                           data1=pwb[:, s * L:(s + 1) * L],
                           initial=av0c[:, j:j + 1], op0=ALU.mult, op1=ALU.add))
                op("vector", lambda e, c0=c0, c1=c1, pA3=pA3: e.tensor_copy(
                    out=carA[:, c0:c1], in_=pA3[:, :, L - 1]))
                op("vector", lambda e, c0=c0, c1=c1, pV3=pV3: e.tensor_copy(
                    out=carAv[:, c0:c1], in_=pV3[:, :, L - 1]))
                for s in range(G):
                    j = g * G + s
                    op("vector", lambda e, s=s, j=j: e.scalar_tensor_tensor(
                        out=pnum[:, s * L + 1:(s + 1) * L],
                        in0=pwb[:, s * L + 1:(s + 1) * L],
                        scalar=tfe8[:, j:j + 1],
                        in1=pVb[:, s * L:(s + 1) * L - 1],
                        op0=ALU.mult, op1=ALU.add))
                    op("vector", lambda e, s=s, j=j: e.scalar_tensor_tensor(
                        out=pnum[:, s * L:s * L + 1],
                        in0=pwb[:, s * L:s * L + 1], scalar=tfe8[:, j:j + 1],
                        in1=av0c[:, j:j + 1], op0=ALU.mult, op1=ALU.add))
                for s in range(G):
                    j = g * G + s
                    op("vector", lambda e, s=s, j=j: e.scalar_tensor_tensor(
                        out=pwb[:, s * L + 1:(s + 1) * L],
                        in0=pu[:, s * L + 1:(s + 1) * L],
                        scalar=tfe8[:, j:j + 1],
                        in1=pAb[:, s * L:(s + 1) * L - 1],
                        op0=ALU.mult, op1=ALU.add))
                    op("vector", lambda e, s=s, j=j: e.scalar_tensor_tensor(
                        out=pwb[:, s * L:s * L + 1],
                        in0=pu[:, s * L:s * L + 1], scalar=tfe8[:, j:j + 1],
                        in1=a0c[:, j:j + 1], op0=ALU.mult, op1=ALU.add),
                       incs=([(sv, 1)] if s == G - 1 else []))
                sv_den = sv.n
                op("scalar", lambda e: e.activation(
                    out=pAb, in_=pwb, func=AF.Ln), waits=[WV(sv, sv_den)],
                   incs=[(ss, 1)])
                op("scalar", lambda e: e.activation(
                    out=pwb, in_=pAb, func=AF.Exp, scale=-1.0),
                   incs=[(ss, 1)])
                state["rden_ss"] = ss.n
                if mid is not None:
                    mid()
                op("vector", lambda e: e.tensor_mul(
                    out=pnum, in0=pnum, in1=pwb),
                   waits=[WV(ss, state["rden_ss"])])
                op("vector", lambda e, c0=c0, c1=c1, srsl=srsl:
                   e.tensor_tensor(out=rwkv[:, c0 * L:c1 * L], in0=pnum,
                                   in1=srsl, op=ALU.mult),
                   incs=[(sv, 1)])
                state[("pu_free", g)] = sv.n
            k_pools.append(pools_thunk)

        k_mm_emit(0)
        k_mm_emit(1)
        k_stats[0]()
        k_mm_emit(2)
        k_stats[1]()
        k_mm_emit(3)
        if ti > 0:
            dnu[2] = load_slot(0, dn_loader(2))
        k_stats[2]()
        k_stats[3]()
        if pipeline_A and ti + 1 < NT:
            a1_next = emit_phaseA1(ti + 1)
        emit_hn(1)
        k_pools[0](mid=lambda: emit_hn(2))
        k_pools[1](mid=lambda: emit_hn(3))
        if pipeline_A and ti + 1 < NT:
            a2i_next = emit_phaseA2a()
        k_pools[2]()
        k_pools[3]()
        sv_wkv = sv.n
        if ti == 0:
            dnu[0] = load_slot(1, dn_loader(0), extra_waits=dn_extra())
            dnu[1] = load_slot(2, dn_loader(1), extra_waits=dn_extra())
            dnu[2] = load_slot(0, dn_loader(2), extra_waits=dn_extra())

        for q in range(NQ):
            if q == 1:
                dnu[3] = load_slot(1, dn_loader(3), extra_waits=dn_extra())
            if q == 2:
                ou = load_attn(2, WOT, WOB, "o")
            dsl, dw = dnu[q]
            for j in [4, 5, 6, 7, 0, 1, 2, 3]:
                bank = pb[j]
                for fi in range(CB):
                    fblk = q * CB + fi
                    wts = []
                    if q == 0 and fi == 0:
                        wts = [WV(sv, sv_hn_q[q]), WV(s_w[dsl], dw),
                               WV(sv, bank_free[j][0]), WV(ss, bank_free[j][1])]
                    elif j == 4 and fi == 0:
                        wts = [WV(sv, sv_hn_q[q]), WV(s_w[dsl], dw)]
                    op("tensor", lambda e, j=j, fi=fi, bank=bank, dsl=dsl,
                       fblk=fblk, q=q: e.matmul(
                           out=bank,
                           lhsT=slots[dsl][:, fi * 1024 + j * P:fi * 1024 + (j + 1) * P],
                           rhs=h[:, fblk * L:(fblk + 1) * L],
                           start=(q == 0 and fi == 0), stop=False),
                       waits=wts,
                       incs=([(st, 1)] if (j == 3 and fi == CB - 1)
                             else []))
            slot_free_at[dsl] = st.n
        osl, ow = ou
        jorder = [6, 5, 4, 3, 2, 7, 0, 1]
        for cb in range(CB):
            cw = [WV(sv, state[("pu_free", cb // 2)])]
            if cb == 0:
                cw.append(WV(s_w[osl], ow))
            for ji, j in enumerate(jorder):
                op("tensor", lambda e, j=j, cb=cb, osl=osl: e.matmul(
                    out=pb[j],
                    lhsT=slots[osl][:, cb * 1024 + j * P:cb * 1024 + (j + 1) * P],
                    rhs=rwkv[:, cb * L:(cb + 1) * L],
                    start=False, stop=(cb == CB - 1)),
                   waits=(cw if ji == 0 else []),
                   incs=([(st, 1)] if cb == CB - 1 else []))
                if cb == CB - 1:
                    state[("st_oj", j)] = st.n
        slot_free_at[osl] = st.n

        def xres_load(ji):
            j = jorder[ji]
            par = ji % 4
            xw = [WV(sv, state["xrso"][par])]
            if ti == 0 and ji < 2:
                xw.append(WV(s_pw, prep.get(("dn", "wdone"), 0)))
            op("sync", lambda e, j=j, par=par, ti=ti: e.dma_start(
                out=xres[par], in_=XFP4[:, ti, j, :]),
               waits=xw, incs=[(s_xr[par], 16)])
            state[("xr", j)] = s_xr[par].n
        for ji in range(4):
            xres_load(ji)

        def evac_one(ji, j):
            par = ji % 4
            ob = ji % 4
            ew = [WV(st, state[("st_oj", j)]),
                  WV(s_xr[par], state[("xr", j)]),
                  WV(s_o[ob], state["oso"][ob])]
            op("vector", lambda e, j=j, par=par, ob=ob: e.tensor_add(
                out=out_sb[ob], in0=pb[j], in1=xres[par]),
               waits=ew, incs=[(sv, 1)])
            sv_ev = sv.n
            state["xrso"][par] = sv_ev
            op("sync", lambda e, j=j, ob=ob, t0=t0: e.dma_start(
                out=OUT[j * P:(j + 1) * P, t0:t0 + L], in_=out_sb[ob]),
               waits=[WV(sv, sv_ev)], incs=[(s_o[ob], 16)])
            state["oso"][ob] = s_o[ob].n
            bank_free[j] = (sv_ev, bank_free[j][1])
            if ji + 4 < CB:
                xres_load(ji + 4)

        for ji in range(3):
            evac_one(ji, jorder[ji])
        if pipeline_A and ti + 1 < NT:
            emit_phaseA2b(ti + 1, a1_next, *a2i_next)
        for ji in range(3, CB):
            evac_one(ji, jorder[ji])

    for s4 in s_o:
        op("sync", lambda e, s4=s4: e.wait_ge(s4.h, s4.n))

    block = ctx.enter_context(nc.Block())

    def mk_body(eng_name):
        def body(eng):
            for waits, incs, fn in ops[eng_name]:
                for s, v in waits:
                    eng.wait_ge(s.h, v)
                instr = fn(eng)
                for s, k in incs:
                    instr.then_inc(s.h, k)
        return body

    block.sync(mk_body("sync"))
    block.gpsimd(mk_body("gpsimd"))
    block.vector(mk_body("vector"))
    block.scalar(mk_body("scalar"))
    block.tensor(mk_body("tensor"))
    ctx.close()
    return nc


def _host_stage(inputs, T, L=512):
    x = np.asarray(inputs["L0_attn_x"], np.float32)
    B = x.shape[0]
    v8 = lambda a: np.ascontiguousarray(
        np.asarray(a, np.float32).reshape(CB, P).T)

    def pmaj(w):
        wt = np.asarray(w, np.float32).T
        cin, co = wt.shape
        return np.ascontiguousarray(
            wt.reshape(cin // P, P, co).transpose(1, 0, 2).reshape(P, -1))

    def pmaj_up(w):
        wt = np.asarray(w, np.float32).T
        return np.ascontiguousarray(
            wt.reshape(CB, P, NQ, FQ).transpose(1, 2, 0, 3).reshape(P, -1))

    common = dict(
        WKT=pmaj(inputs["L0_attn_w_k"]),
        WVT=pmaj(inputs["L0_attn_w_v"]),
        WRT=pmaj(inputs["L0_attn_w_r"]),
        WOT=pmaj(inputs["L0_attn_w_o"]),
        WUT=pmaj_up(inputs["L0_ffn_w_up"]),
        WDT=pmaj(inputs["L0_ffn_w_down"]),
        TD8=v8(inputs["L0_attn_time_decay"]),
        TF8=v8(inputs["L0_attn_time_first"]),
        MK8=v8(inputs["L0_attn_time_mix_key"]),
        MV8=v8(inputs["L0_attn_time_mix_value"]),
        MR8=v8(inputs["L0_attn_time_mix_receptance"]),
    )
    in_maps = []
    for b in range(B):
        m = dict(common)
        xc = np.ascontiguousarray(x[b, :T].T)
        m["XF"] = np.ascontiguousarray(
            xc.reshape(CB, P, T // L, L).transpose(1, 2, 0, 3).reshape(P, -1))
        in_maps.append(m)
    return in_maps


def kernel(**inputs):
    T = 4096
    in_maps = _host_stage(inputs, T)
    nc = build_program(T=T, L=512)
    res = run_bass_kernel_spmd(nc, in_maps, core_ids=list(range(len(in_maps))))
    outs = [np.ascontiguousarray(np.asarray(r["OUT"]).T) for r in res.results]
    return np.stack(outs, axis=0)


# version 21
# speedup vs baseline: 1.0048x; 1.0048x over previous
import numpy as np
import ml_dtypes
import concourse.bass as bass
import concourse.mybir as mybir
from concourse.bass_utils import run_bass_kernel_spmd
from contextlib import ExitStack

F32 = mybir.dt.float32
BF16 = mybir.dt.bfloat16
AF = mybir.ActivationFunctionType
ALU = mybir.AluOpType
AXX = mybir.AxisListType.X

P, C, DFF = 128, 1024, 4096
CB, FB = C // P, DFF // P
NG, G = 4, 2
NQ = 4
FQ = DFF // NQ


class _Sem:
    __slots__ = ("h", "n")
    def __init__(self, h):
        self.h = h
        self.n = 0


def build_program(T=4096, L=512, pipeline_A=True, s_rden=True):
    NT = T // L
    nc = bass.Bass("TRN2", target_bir_lowering=False,
                   detect_race_conditions=False)

    XF = nc.dram_tensor("XF", [P, T * CB], F32, kind="ExternalInput").ap()
    WKT = nc.dram_tensor("WKT", [P, CB * C], F32, kind="ExternalInput").ap()
    WVT = nc.dram_tensor("WVT", [P, CB * C], F32, kind="ExternalInput").ap()
    WRT = nc.dram_tensor("WRT", [P, CB * C], F32, kind="ExternalInput").ap()
    WOT = nc.dram_tensor("WOT", [P, CB * C], F32, kind="ExternalInput").ap()
    WUT = nc.dram_tensor("WUT", [P, CB * DFF], F32, kind="ExternalInput").ap()
    WDT = nc.dram_tensor("WDT", [P, CB * DFF], F32, kind="ExternalInput").ap()
    TD8 = nc.dram_tensor("TD8", [P, CB], F32, kind="ExternalInput").ap()
    TF8 = nc.dram_tensor("TF8", [P, CB], F32, kind="ExternalInput").ap()
    MK8 = nc.dram_tensor("MK8", [P, CB], F32, kind="ExternalInput").ap()
    MV8 = nc.dram_tensor("MV8", [P, CB], F32, kind="ExternalInput").ap()
    MR8 = nc.dram_tensor("MR8", [P, CB], F32, kind="ExternalInput").ap()
    OUT = nc.dram_tensor("OUT", [C, T], F32, kind="ExternalOutput").ap()
    UPB = nc.dram_tensor("UPB", [P, CB * DFF], BF16, kind="Internal").ap()
    DNB = nc.dram_tensor("DNB", [P, CB * DFF], BF16, kind="Internal").ap()
    WKB = nc.dram_tensor("WKB", [P, CB * C], BF16, kind="Internal").ap()
    WVB = nc.dram_tensor("WVB", [P, CB * C], BF16, kind="Internal").ap()
    WRB = nc.dram_tensor("WRB", [P, CB * C], BF16, kind="Internal").ap()
    WOB = nc.dram_tensor("WOB", [P, CB * C], BF16, kind="Internal").ap()

    XFP4 = XF.rearrange("p (n c l) -> p n c l", n=NT, c=CB)

    ctx = ExitStack()
    sb = lambda n, s, d=F32: ctx.enter_context(nc.sbuf_tensor(n, s, d)).ap()
    ps = lambda n, s: ctx.enter_context(nc.psum_tensor(n, s, F32)).ap()
    mksem = lambda n: _Sem(ctx.enter_context(nc.semaphore(n)))

    ones_f = sb("ones_f", [P, P])
    ones_b = sb("ones_b", [P, P], BF16)
    td8 = sb("td8", [P, CB]); tf8 = sb("tf8", [P, CB]); dcol = sb("dcol", [P, CB])
    mk8 = sb("mk8", [P, CB]); mv8 = sb("mv8", [P, CB]); mr8 = sb("mr8", [P, CB])
    omk8 = sb("omk8", [P, CB]); omv8 = sb("omv8", [P, CB]); omr8 = sb("omr8", [P, CB])
    sprev = sb("sprev", [P, CB]); snew = sb("snew", [P, CB])
    m8 = sb("m8", [P, CB]); d8 = sb("d8", [P, CB]); ns8 = sb("ns8", [P, CB])
    tfe8 = sb("tfe8", [P, CB]); resc8 = sb("resc8", [P, CB])
    carA = sb("carA", [P, CB]); carAv = sb("carAv", [P, CB])
    a0c = sb("a0c", [P, CB]); av0c = sb("av0c", [P, CB])
    xprev = sb("xprev", [P, CB], BF16)
    thrU = sb("thrU", [P, 1]); nthrU = sb("nthrU", [P, 1])
    ln05 = sb("ln05", [P, 1])
    thrD = sb("thrD", [P, 1]); nthrD = sb("nthrD", [P, 1])
    CEL = 2 * L
    NCH = C * DFF // (P * CEL)
    accW = sb("accW", [P, 2 * NCH]); accT = sb("accT", [P, 1])

    x_sb = sb("x_sb", [P, CB * L])
    xn = sb("xn", [P, CB * L], BF16)
    xmA = sb("xmA", [P, CB * L], BF16)
    xmB = sb("xmB", [P, CB * L], BF16)
    v_bf = sb("v_bf", [P, CB * L], BF16)
    sr = sb("sr", [P, CB * L], BF16)
    rwkv = sb("rwkv", [P, CB * L], BF16)
    h = sb("h", [P, FB * L], BF16)
    pu4 = [sb(f"pu{i}", [P, G * L]) for i in range(NG)]
    pwb = sb("pwb", [P, G * L])
    pAb = sb("pAb", [P, G * L])
    pVb = sb("pVb", [P, G * L])
    pnum = sb("pnum", [P, G * L])
    mean = sb("mean", [P, L])
    rxn05 = sb("rxn05", [P, L]); rh05 = sb("rh05", [P, L])
    xsq = [sb(f"xsq{i}", [P, L]) for i in range(2)]
    out_sb = [sb(f"outsb{i}", [P, L]) for i in range(4)]
    xres = [sb(f"xres{i}", [P, L]) for i in range(4)]
    hsq = [sb(f"hsq{i}", [P, L], BF16) for i in range(2)]
    slots = [sb(f"slot{i}", [P, CB * 1024], BF16) for i in range(3)]

    pb_full = [ps(f"pb{i}", [P, 512]) for i in range(8)]
    pb = [t[:, 0:L] for t in pb_full]

    x3 = x_sb.rearrange("p (cb l) -> p cb l", cb=CB)
    xn3 = xn.rearrange("p (cb l) -> p cb l", cb=CB)
    xmA3 = xmA.rearrange("p (cb l) -> p cb l", cb=CB)
    xmB3 = xmB.rearrange("p (cb l) -> p cb l", cb=CB)

    mean_b = mean.unsqueeze(1).broadcast_to((P, CB, L))
    rstd_b = xsq[1].unsqueeze(1).broadcast_to((P, CB, L))
    rxn_b = rxn05.unsqueeze(1).broadcast_to((P, CB, L))

    s_ld = mksem("s_ld")
    s_x = mksem("s_x")
    s_xr = [mksem(f"s_xr{i}") for i in range(4)]
    s_o = [mksem(f"s_o{i}") for i in range(4)]
    s_w = [mksem(f"s_w{i}") for i in range(3)]
    s_p = [mksem(f"s_p{i}") for i in range(8)]
    s_pw = mksem("s_pw")
    st = mksem("st")
    sv = mksem("sv")
    ss = mksem("ss")
    sg = mksem("sg")

    ops = {e: [] for e in ("sync", "gpsimd", "vector", "scalar", "tensor")}

    def op(eng, fn, waits=(), incs=()):
        rec = []
        for s, k in incs:
            s.n += k
            rec.append((s, k))
        ops[eng].append(([(s, v) for s, v in waits if v > 0], rec, fn))

    def W(s):
        return (s, s.n)

    def WV(s, v):
        return (s, v)

    op("sync", lambda e: e.dma_start(out=td8, in_=TD8), incs=[(s_ld, 16)])
    op("sync", lambda e: e.dma_start(out=tf8, in_=TF8), incs=[(s_ld, 16)])
    op("sync", lambda e: e.dma_start(out=mk8, in_=MK8), incs=[(s_ld, 16)])
    op("sync", lambda e: e.dma_start(out=mv8, in_=MV8), incs=[(s_ld, 16)])
    op("sync", lambda e: e.dma_start(out=mr8, in_=MR8), incs=[(s_ld, 16)])

    op("vector", lambda e: e.memset(ones_f, 1.0))
    op("vector", lambda e: e.memset(ones_b, 1.0))
    op("vector", lambda e: e.memset(sprev, -1.0e30))
    op("vector", lambda e: e.memset(carA, 0.0))
    op("vector", lambda e: e.memset(carAv, 0.0))
    op("vector", lambda e: e.memset(ln05, -0.6931471805599453))
    op("vector", lambda e: e.memset(xprev, 0.0), incs=[(sv, 1)])
    sv_setup = sv.n
    op("vector", lambda e: e.tensor_scalar(out=omk8, in0=mk8, scalar1=-1.0,
                                           scalar2=1.0, op0=ALU.mult, op1=ALU.add),
       waits=[W(s_ld)])
    op("vector", lambda e: e.tensor_scalar(out=omv8, in0=mv8, scalar1=-1.0,
                                           scalar2=1.0, op0=ALU.mult, op1=ALU.add))
    op("vector", lambda e: e.tensor_scalar(out=omr8, in0=mr8, scalar1=-1.0,
                                           scalar2=1.0, op0=ALU.mult, op1=ALU.add),
       incs=[(sv, 1)])
    sv_omix = sv.n
    op("scalar", lambda e: e.activation(out=tfe8, in_=tf8, func=AF.Exp),
       waits=[W(s_ld)], incs=[(ss, 1)])
    op("scalar", lambda e: e.activation(out=dcol, in_=td8, func=AF.Exp),
       incs=[(ss, 1)])
    op("scalar", lambda e: e.activation(out=dcol, in_=dcol, func=AF.Exp, scale=-1.0),
       waits=[W(ss)], incs=[(ss, 1)])
    ss_dcol = ss.n

    slot_free_at = [0, 0, 0]
    slot_pw_gate = [0, 0, 0]

    def load_slot(sl, fn_builder, extra_waits=()):
        waits = [WV(st, slot_free_at[sl])] + list(extra_waits)
        if slot_pw_gate[sl]:
            waits.append(WV(s_pw, slot_pw_gate[sl]))
            slot_pw_gate[sl] = 0
        op("gpsimd", fn_builder(slots[sl]), waits=waits, incs=[(s_w[sl], 16)])
        return sl, s_w[sl].n

    def attn_w_loader(src):
        def b(slot):
            return lambda e: e.dma_start(out=slot, in_=src)
        return b

    def load_attn(sl_pref, wf32, wbf, key, extra_waits=()):
        if key not in attn_bf:
            sl, cnt = load_slot(sl_pref, attn_w_loader(wf32),
                                extra_waits=extra_waits)
            op("gpsimd", lambda e, sl=sl, wbf=wbf: e.dma_start(
                out=wbf, in_=slots[sl]), incs=[(s_pw, 16)])
            attn_bf[key] = s_pw.n
            slot_pw_gate[sl] = s_pw.n
            return sl, cnt
        ex = [WV(s_pw, attn_bf[key])] + list(extra_waits)
        return load_slot(sl_pref, attn_w_loader(wbf), extra_waits=ex)

    attn_bf = {}

    def up_loader(q):
        def b(slot):
            return lambda e: e.dma_start(
                out=slot, in_=UPB[:, q * CB * FQ:(q + 1) * CB * FQ])
        return b

    def dn_loader(q):
        def b(slot):
            return lambda e: e.dma_start(
                out=slot, in_=DNB[:, q * CB * C:(q + 1) * CB * C])
        return b

    prep = {}

    h_f32 = h.bitcast(F32)
    ubufs = [h_f32[:, i * CEL:(i + 1) * CEL] for i in range(8)]
    ubuf_t = [h[:, 2 * i * CEL:2 * i * CEL + CEL] for i in range(8)]
    dbufs = [pwb, pAb, pVb, pnum] + pu4
    dbuf_t = [t.bitcast(BF16)[:, 0:CEL]
              for t in (pwb, pAb, pVb, pnum)] + \
             [t.bitcast(BF16)[:, 0:CEL] for t in pu4]
    scr_u = [sr[:, i * CEL:(i + 1) * CEL] for i in range(4)]
    scr_d = [rwkv[:, i * CEL:(i + 1) * CEL] for i in range(4)]

    def _pp(which):
        if which == "up":
            return WUT, UPB, ubufs, scr_u, thrU, nthrU, 0, ubuf_t
        return WDT, DNB, dbufs, scr_d, thrD, nthrD, NCH, dbuf_t

    def _prep_load(which, ci, ph):
        wsrc, _, bufs, _, _, _, _, _ = _pp(which)
        if ci >= NCH:
            return
        b = ci % len(bufs)
        si = (0 if which == "up" else 4) + ci % 4
        waits = []
        lu = prep.get((which, "lastuse", b), None)
        if lu is not None:
            waits.append(lu)
        if ph == 0:
            if ci >= 4:
                waits.append(WV(ss, prep[(which, "ause", ci - 4)]))
        else:
            waits.append(WV(ss, prep[(which, "tuse", ci - 4)] if ci >= 4
                            else prep[(which, "ause", NCH - 4 + ci)]))
        op("gpsimd", lambda e, ci=ci, b=b, bufs=bufs, wsrc=wsrc:
           e.dma_start(out=bufs[b], in_=wsrc[:, ci * CEL:(ci + 1) * CEL]),
           waits=waits, incs=[(s_p[si], 16)])
        prep[(which, "ld", ci)] = (si, s_p[si].n)

    def emit_prep_alpha_chunk(which, ci):
        _, _, bufs, _, _, _, acol, _ = _pp(which)
        nb = len(bufs)
        if ci == 0:
            for c2 in range(min(4, NCH)):
                _prep_load(which, c2, 0)
        b = ci % nb
        si, cnt = prep[(which, "ld", ci)]
        op("scalar", lambda e, b=b, ci=ci, bufs=bufs, acol=acol:
           e.activation(out=bufs[b], in_=bufs[b], func=AF.Abs,
                        accum_out=accW[:, acol + ci:acol + ci + 1]),
           waits=[WV(s_p[si], cnt)], incs=[(ss, 1)])
        prep[(which, "ause", ci)] = ss.n
        prep[(which, "lastuse", b)] = WV(ss, ss.n)
        _prep_load(which, ci + 4, 0)

    def emit_prep_alpha_fin(which):
        acol = _pp(which)[6]
        op("vector", lambda e, acol=acol:
           e.tensor_reduce(out=accT, in_=accW[:, acol:acol + NCH],
                           axis=AXX, op=ALU.add),
           waits=[W(ss)], incs=[(sv, 1)])
        sv_acc = sv.n
        op("tensor", lambda e: e.matmul(out=pb[5][:, 0:1], lhsT=ones_f,
                                        rhs=accT, start=True, stop=True),
           waits=[WV(sv, sv_acc), WV(sv, sv_setup),
                  WV(sv, bank_free[5][0]), WV(ss, bank_free[5][1])],
           incs=[(st, 1)])
        st_mm = st.n
        thr = thrU if which == "up" else thrD
        nthr = nthrU if which == "up" else nthrD
        sc = 0.5 / float(C * DFF)
        op("scalar", lambda e, thr=thr, sc=sc:
           e.mul(out=thr, in_=pb[5][:, 0:1], mul=sc),
           waits=[WV(st, st_mm)], incs=[(ss, 1)])
        op("scalar", lambda e, nthr=nthr, sc=sc:
           e.mul(out=nthr, in_=pb[5][:, 0:1], mul=-sc), incs=[(ss, 1)])
        prep[(which, "thr")] = ss.n
        bank_free[5] = (0, ss.n)

    def emit_prep_tern_chunk(which, ci, oi):
        _, wdst, bufs, scr, thr, nthr, _, tbufs = _pp(which)
        nb = len(bufs)
        if oi == 0:
            for c2 in range(min(4, NCH)):
                _prep_load(which, c2, 1)
        b = ci % nb
        s1 = scr[2 * (oi % 2)]
        s2 = scr[2 * (oi % 2) + 1]
        si, cnt = prep[(which, "ld", ci)]
        wts = [WV(ss, prep[(which, "thr")]),
               WV(s_p[si], cnt),
               WV(sv, prep.get((which, "tadd", oi - 2), 0))]
        op("scalar", lambda e, b=b, s1=s1, thr=thr, bufs=bufs:
           e.sign(out=s1, in_=bufs[b], bias=thr), waits=wts)
        op("scalar", lambda e, b=b, s2=s2, nthr=nthr, bufs=bufs:
           e.sign(out=s2, in_=bufs[b], bias=nthr), incs=[(ss, 1)])
        prep[(which, "tuse", ci)] = ss.n
        op("vector", lambda e, b=b, s1=s1, s2=s2, tbufs=tbufs:
           e.tensor_add(out=tbufs[b], in0=s1, in1=s2),
           waits=[W(ss)], incs=[(sv, 1)])
        prep[(which, "tadd", oi)] = sv.n
        op("gpsimd", lambda e, ci=ci, b=b, wdst=wdst, tbufs=tbufs:
           e.dma_start(out=wdst[:, ci * CEL:(ci + 1) * CEL], in_=tbufs[b]),
           waits=[WV(sv, sv.n)], incs=[(s_pw, 16)])
        prep[(which, "lastuse", b)] = WV(s_pw, s_pw.n)
        _prep_load(which, ci + 4, 1)

    def _done(which):
        prep[(which, "wdone")] = s_pw.n

    prep_q = []
    for ci in range(NCH):
        prep_q.append(lambda ci=ci: emit_prep_alpha_chunk("up", ci))
        prep_q.append(lambda ci=ci: emit_prep_alpha_chunk("dn", ci))
    prep_q.append(lambda: emit_prep_alpha_fin("up"))
    prep_q.append(lambda: emit_prep_alpha_fin("dn"))
    prep_q += [lambda ci=ci: emit_prep_tern_chunk("up", ci, ci)
               for ci in range(NCH)]
    prep_q.append(lambda: _done("up"))
    prep_q += [lambda ci=ci: emit_prep_tern_chunk("dn", ci, ci)
               for ci in range(NCH)]
    prep_q.append(lambda: _done("dn"))
    dn_q = []

    def drain(q, n):
        for _ in range(min(n, len(q))):
            q.pop(0)()

    bank_free = [(0, 0) for _ in range(8)]
    state = {"wkv_sv": {}, "hsq_st": {}, "xsq_st": {}, "oso": [0, 0, 0, 0],
             "xA": None, "phA": {}, "st_up": 0,
             "st_r": 0, "st_v": 0, "st_k": 0, "xrso": [0, 0, 0, 0]}

    def emit_phaseA1(ti):
        t0 = ti * L
        xw = []
        if state["xA"] is not None:
            a = state["xA"]
            xw = [WV(st, a["st"]), WV(ss, a["ss"]), WV(sv, a["sv"]),
                  WV(s_pw, state.pop("xA_pw", 0))]
        op("sync", lambda e, ti=ti: e.dma_start(out=x3, in_=XFP4[:, ti]),
           waits=xw, incs=[(s_x, 16)])
        sx_here = s_x.n
        for cb in range(CB):
            wts = []
            if cb == 0:
                wts = [WV(s_x, sx_here), WV(sv, sv_setup),
                       WV(sv, bank_free[7][0]), WV(ss, bank_free[7][1])]
            op("tensor", lambda e, cb=cb: e.matmul(
                out=pb[7], lhsT=ones_f, rhs=x3[:, cb, :],
                start=(cb == 0), stop=(cb == CB - 1)),
               waits=wts, incs=([(st, 1)] if cb == CB - 1 else []))
        st_sx = st.n
        op("vector", lambda e: e.tensor_scalar_mul(out=mean, in0=pb[7],
                                                   scalar1=1.0 / C),
           waits=[WV(st, st_sx)], incs=[(sv, 1)])
        sv_mean = sv.n
        for cb in range(CB):
            par = cb % 2
            op("scalar", lambda e, cb=cb, par=par: e.activation(
                out=xsq[par], in_=x3[:, cb, :], func=AF.Square),
               waits=[WV(s_x, sx_here), WV(st, state["xsq_st"].get(par, 0))],
               incs=[(ss, 1)])
            ss_xsq = ss.n
            wts = [WV(ss, ss_xsq)]
            if cb == 0:
                wts += [WV(sv, sv_mean)]
            op("tensor", lambda e, par=par, cb=cb: e.matmul(
                out=pb[7], lhsT=ones_f, rhs=xsq[par],
                start=(cb == 0), stop=(cb == CB - 1)),
               waits=wts, incs=[(st, 1)])
            state["xsq_st"][par] = st.n
        st_sx2 = st.n
        ss_sq = ss.n
        op("vector", lambda e: e.tensor_scalar_mul(out=xsq[0], in0=pb[7],
                                                   scalar1=1.0 / C),
           waits=[WV(st, st_sx2)], incs=[(sv, 1)])
        sv_b67 = sv.n
        bank_free[7] = (sv_b67, 0)
        return dict(sx=sx_here, sv_b67=sv_b67, st_sx2=st_sx2, ss_sq=ss_sq)

    def emit_phaseA2a():
        op("vector", lambda e: e.tensor_mul(out=xsq[1], in0=mean, in1=mean))
        op("vector", lambda e: e.tensor_sub(out=xsq[0], in0=xsq[0], in1=xsq[1]))
        op("vector", lambda e: e.tensor_scalar_add(out=xsq[1], in0=xsq[0],
                                                   scalar1=1.0e-5),
           incs=[(sv, 1)])
        sv_vpe = sv.n
        op("scalar", lambda e: e.activation(out=xsq[1], in_=xsq[1], func=AF.Ln),
           waits=[WV(sv, sv_vpe)], incs=[(ss, 1)])
        op("scalar", lambda e: e.activation(out=rh05, in_=xsq[0], func=AF.Ln),
           waits=[WV(sv, state.get("sv_hn", 0))],
           incs=[(ss, 1)])
        ss_ln = ss.n
        op("vector", lambda e: e.tensor_sub(out=rh05, in0=rh05, in1=xsq[1]),
           waits=[WV(ss, ss_ln)], incs=[(sv, 1)])
        op("scalar", lambda e: e.activation(out=rh05, in_=rh05, func=AF.Exp,
                                            scale=0.5), waits=[W(sv)],
           incs=[(ss, 1)])
        op("scalar", lambda e: e.activation(out=xsq[1], in_=xsq[1], func=AF.Exp,
                                            scale=-0.5), incs=[(ss, 1)])
        ss_e = ss.n
        op("vector", lambda e: e.tensor_scalar_add(out=rh05, in0=rh05,
                                                   scalar1=1.0e-8),
           waits=[WV(ss, ss_e)], incs=[(sv, 1)])
        op("scalar", lambda e: e.activation(out=rh05, in_=rh05, func=AF.Ln),
           waits=[W(sv)], incs=[(ss, 1)])
        op("scalar", lambda e: e.activation(out=rxn05, in_=rh05, func=AF.Exp,
                                            scale=-1.0, bias=ln05),
           incs=[(ss, 1)])
        return ss_e, ss.n

    def emit_phaseA2b(ti, a1, ss_e, ss_rxn):
        xnn_cb = []
        for cb in range(0, CB, 2):
            wts = ([WV(st, state["st_v"]), WV(ss, ss_e)] if cb == 0 else [])
            op("vector", lambda e, cb=cb: e.tensor_sub(
                out=xmB3[:, cb:cb + 2, :], in0=x3[:, cb:cb + 2, :],
                in1=mean_b[:, cb:cb + 2, :]), waits=wts)
            op("vector", lambda e, cb=cb: e.tensor_tensor(
                out=xn3[:, cb:cb + 2, :], in0=xmB3[:, cb:cb + 2, :],
                in1=rstd_b[:, cb:cb + 2, :], op=ALU.mult))
            op("vector", lambda e, cb=cb: e.tensor_tensor(
                out=xmB3[:, cb:cb + 2, :], in0=xn3[:, cb:cb + 2, :],
                in1=rxn_b[:, cb:cb + 2, :], op=ALU.mult),
               waits=([WV(ss, ss_rxn)] if cb == 0 else []), incs=[(sv, 1)])
            xnn_cb += [sv.n, sv.n]
        sv_xnn = sv.n
        state["phA"][ti] = dict(sv_xnn=sv_xnn, xnn_cb=xnn_cb, **a1)
        state["xA"] = dict(st=a1["st_sx2"], ss=a1["ss_sq"], sv=sv_xnn)

    def emit_lerp(mix, omix, dst3, extra_w=()):
        for cb in range(CB):
            wts = list(extra_w) if cb == 0 else []
            op("vector", lambda e, cb=cb, mix=mix, dst3=dst3:
               e.tensor_scalar_mul(out=dst3[:, cb, :], in0=xn3[:, cb, :],
                                   scalar1=mix[:, cb:cb + 1]), waits=wts)
            op("vector", lambda e, cb=cb, omix=omix, dst3=dst3:
               e.scalar_tensor_tensor(
                   out=dst3[:, cb, 1:L], in0=xn3[:, cb, 0:L - 1],
                   scalar=omix[:, cb:cb + 1], in1=dst3[:, cb, 1:L],
                   op0=ALU.mult, op1=ALU.add))
        op("vector", lambda e, omix=omix: e.tensor_tensor(
            out=d8, in0=xprev, in1=omix, op=ALU.mult),
           waits=[WV(ss, state.get("resc_ss", 0))], incs=[(sv, 1)])
        op("vector", lambda e, dst3=dst3: e.tensor_tensor(
            out=dst3[:, :, 0], in0=dst3[:, :, 0], in1=d8, op=ALU.add),
           waits=[W(sv)], incs=[(sv, 1)])
        return sv.n

    for ti in range(NT):
        t0 = ti * L
        if ti == 0 or not pipeline_A:
            a1 = emit_phaseA1(ti)
            a2i = emit_phaseA2a()
            emit_phaseA2b(ti, a1, *a2i)
        phA = state["phA"][ti]

        if ti == 0:
            drain(prep_q, 2 * NCH + 2 + NCH + 1)
        sv_lr = emit_lerp(mr8, omr8, xmA3, extra_w=[WV(st, state["st_k"])])

        up_extra = ([WV(s_pw, prep.get(("up", "wdone"), 0))] if ti == 0 else [])
        upu = [None] * NQ
        upu[0] = load_slot(0, up_loader(0), extra_waits=up_extra)
        upu[1] = load_slot(1, up_loader(1), extra_waits=up_extra)
        upu[2] = load_slot(2, up_loader(2), extra_waits=up_extra)

        ubanks = [4, 5, 2, 3]
        for q in range(NQ):
            if q == 1:
                upu[3] = load_slot(0, up_loader(3), extra_waits=up_extra)
            if q == 2:
                ru = load_attn(1, WRT, WRB, "r")
            if q == 3:
                vu = load_attn(2, WVT, WVB, "v")
            usl, uw = upu[q]

            def emit_sh2(g):
                if g < 0:
                    return
                wts = [WV(ss, state[("sh2ss", g)])]
                if g == 0:
                    wts += [WV(sv, bank_free[6][0]), WV(ss, bank_free[6][1]),
                            WV(sv, sv_setup)]
                op("tensor", lambda e, g=g: e.matmul(
                    out=pb[6], lhsT=ones_b, rhs=hsq[g % 2],
                    start=(g == 0), stop=(g == FB - 1)),
                   waits=wts, incs=[(st, 1)])
                state["hsq_st"][g % 2] = st.n

            for fi in range(FQ // P):
                gblk = q * (FQ // P) + fi
                ub = ubanks[gblk % 4]
                bank = pb[ub]
                for cb in range(CB):
                    wts = []
                    if gblk == 0:
                        wts = [WV(sv, phA["xnn_cb"][cb])]
                    if cb == 0:
                        wts += [WV(s_w[usl], uw),
                                WV(sv, bank_free[ub][0]),
                                WV(ss, bank_free[ub][1])]
                    op("tensor", lambda e, cb=cb, fi=fi, bank=bank, usl=usl:
                       e.matmul(
                           out=bank,
                           lhsT=slots[usl][:, cb * FQ + fi * P:cb * FQ + (fi + 1) * P],
                           rhs=xmB3[:, cb, :], start=(cb == 0), stop=(cb == CB - 1)),
                       waits=wts, incs=([(st, 1)] if cb == CB - 1 else []))
                st_up = st.n
                emit_sh2(gblk - 2)
                op("scalar", lambda e, gblk=gblk, bank=bank: e.activation(
                    out=h[:, gblk * L:(gblk + 1) * L], in_=bank, func=AF.Gelu),
                   waits=[WV(st, st_up)], incs=[(ss, 1)])
                op("scalar", lambda e, gblk=gblk: e.activation(
                    out=hsq[gblk % 2], in_=h[:, gblk * L:(gblk + 1) * L],
                    func=AF.Square),
                   waits=[WV(st, state["hsq_st"].get(gblk % 2, 0))],
                   incs=[(ss, 1)])
                state[("sh2ss", gblk)] = ss.n
                bank_free[ub] = (0, ss.n)
            slot_free_at[usl] = st.n
        emit_sh2(FB - 2)
        emit_sh2(FB - 1)
        ku = load_attn(0, WKT, WKB, "k")
        if ti == 0:
            drain(prep_q, len(prep_q))
        st_sh2 = st.n
        state["st_up"] = st_sh2

        op("vector", lambda e: e.tensor_scalar_mul(out=rh05, in0=pb[6],
                                                   scalar1=1.0 / DFF),
           waits=[WV(st, st_sh2)], incs=[(sv, 1)])
        bank_free[6] = (sv.n, 0)
        op("scalar", lambda e: e.activation(out=rh05, in_=rh05, func=AF.Ln),
           waits=[W(sv)], incs=[(ss, 1)])
        op("scalar", lambda e: e.activation(out=rh05, in_=rh05, func=AF.Exp,
                                            scale=0.5), incs=[(ss, 1)])
        op("vector", lambda e: e.tensor_scalar_add(out=rh05, in0=rh05,
                                                   scalar1=1.0e-8),
           waits=[W(ss)], incs=[(sv, 1)])
        op("scalar", lambda e: e.activation(out=rh05, in_=rh05, func=AF.Ln),
           waits=[W(sv)], incs=[(ss, 1)])
        op("scalar", lambda e: e.activation(out=rh05, in_=rh05, func=AF.Exp,
                                            scale=-1.0, bias=ln05),
           incs=[(ss, 1)])
        ss_rh = ss.n
        sv_hn_q = [0] * NQ

        def emit_hn(qq):
            for gi in range(FB // NQ):
                gblk = qq * (FB // NQ) + gi
                wts = [WV(ss, ss_rh)] if gblk == 0 else []
                op("vector", lambda e, gblk=gblk: e.tensor_tensor(
                    out=h[:, gblk * L:(gblk + 1) * L],
                    in0=h[:, gblk * L:(gblk + 1) * L], in1=rh05, op=ALU.mult),
                   waits=wts,
                   incs=([(sv, 1)] if gi == FB // NQ - 1 else []))
            sv_hn_q[qq] = sv.n
            if qq == NQ - 1:
                state["sv_hn"] = sv.n

        def proj(slw, src3, sv_src, evac):
            sl, wld = slw
            for j in range(CB):
                bank = pb[j % 2]
                for cb in range(CB):
                    wts = []
                    if cb == 0:
                        wts = [WV(sv, sv_src), WV(s_w[sl], wld),
                               WV(sv, bank_free[j % 2][0]),
                               WV(ss, bank_free[j % 2][1])]
                    op("tensor", lambda e, j=j, cb=cb, bank=bank, sl=sl: e.matmul(
                        out=bank,
                        lhsT=slots[sl][:, cb * 1024 + j * P:cb * 1024 + (j + 1) * P],
                        rhs=src3[:, cb, :], start=(cb == 0), stop=(cb == CB - 1)),
                       waits=wts, incs=([(st, 1)] if cb == CB - 1 else []))
                evac(j, bank, st.n)
            slot_free_at[sl] = st.n
            return st.n

        def evac_r(j, bank, st_c):
            pw = ([WV(s_pw, prep.get(("up", "wdone"), 0))] if j == 0 else [])
            op("scalar", lambda e, j=j, bank=bank: e.activation(
                out=sr[:, j * L:(j + 1) * L], in_=bank, func=AF.Sigmoid),
               waits=[WV(st, st_c)] + pw, incs=[(ss, 1)])
            bank_free[j % 2] = (0, ss.n)

        def evac_v(j, bank, st_c):
            op("scalar", lambda e, j=j, bank=bank: e.activation(
                out=v_bf[:, j * L:(j + 1) * L], in_=bank, func=AF.Copy),
               waits=[WV(st, st_c)], incs=[(ss, 1)])
            bank_free[j % 2] = (0, ss.n)

        state["st_r"] = proj(ru, xmA3, sv_lr, evac_r)
        dnu = [None] * NQ

        def dn_extra():
            return ([WV(s_pw, prep.get(("dn", "wdone"), 0))] if ti == 0 else [])
        if ti > 0:
            dnu[0] = load_slot(1, dn_loader(0))

        sv_lv = emit_lerp(mv8, omv8, xmB3, extra_w=[WV(st, state["st_up"])])
        state["st_v"] = proj(vu, xmB3, sv_lv, evac_v)
        if ti > 0:
            dnu[1] = load_slot(2, dn_loader(1))

        sv_lk = emit_lerp(mk8, omk8, xmA3, extra_w=[WV(st, state["st_r"])])
        op("vector", lambda e: e.tensor_copy(out=xprev, in_=xn3[:, :, L - 1]),
           incs=[(sv, 1)])
        emit_hn(0)
        ksl, kw = ku

        def k_mm_emit(g):
            kb = (g % 2) * 2
            for s in range(G):
                j = g * G + s
                bank = pb[kb + s]
                for cb in range(CB):
                    wts = []
                    if cb == 0:
                        wts = [WV(sv, sv_lk), WV(s_w[ksl], kw),
                               WV(sv, bank_free[kb + s][0]),
                               WV(ss, bank_free[kb + s][1])]
                    op("tensor", lambda e, j=j, cb=cb, bank=bank, ksl=ksl: e.matmul(
                        out=bank,
                        lhsT=slots[ksl][:, cb * 1024 + j * P:cb * 1024 + (j + 1) * P],
                        rhs=xmA3[:, cb, :], start=(cb == 0), stop=(cb == CB - 1)),
                       waits=wts, incs=([(st, 1)] if cb == CB - 1 else []))
                state[("st_kj", g, s)] = st.n
            state["st_k"] = st.n
            if g == NG - 1:
                slot_free_at[ksl] = st.n

        k_stats = []
        k_pools = []
        for g in range(NG):
            kb = (g % 2) * 2

            def stats_thunk(g=g, kb=kb, ti=ti):
                stk = (state[("st_kj", g, 0)], state[("st_kj", g, 1)])
                pu = pu4[g]
                c0, c1 = g * G, g * G + G
                op("vector", lambda e, c0=c0, kb=kb: e.tensor_reduce(
                    out=m8[:, c0:c0 + 1], in_=pb[kb], axis=AXX, op=ALU.max),
                   waits=[WV(st, stk[0])])
                op("vector", lambda e, c0=c0, kb=kb: e.tensor_reduce(
                    out=m8[:, c0 + 1:c0 + 2], in_=pb[kb + 1], axis=AXX,
                    op=ALU.max),
                   waits=[WV(st, stk[1])], incs=[(sv, 1)])
                op("vector", lambda e, c0=c0, c1=c1: e.tensor_max(
                    out=snew[:, c0:c1], in0=m8[:, c0:c1], in1=sprev[:, c0:c1]),
                   waits=[W(sv)], incs=[(sv, 1)])
                op("vector", lambda e, c0=c0, c1=c1: e.tensor_sub(
                    out=d8[:, c0:c1], in0=sprev[:, c0:c1], in1=snew[:, c0:c1]),
                   waits=[W(sv)])
                op("vector", lambda e, c0=c0, c1=c1: e.tensor_scalar_mul(
                    out=ns8[:, c0:c1], in0=snew[:, c0:c1], scalar1=-1.0))
                op("vector", lambda e, c0=c0, c1=c1: e.tensor_copy(
                    out=sprev[:, c0:c1], in_=snew[:, c0:c1]), incs=[(sv, 1)])
                sv_stats = sv.n
                pg = []
                if ti == 0:
                    pg = [WV(s_pw, max(prep.get(("up", "wdone"), 0),
                                       prep.get(("dn", "wdone"), 0)))]
                op("scalar", lambda e, c0=c0, c1=c1: e.activation(
                    out=resc8[:, c0:c1], in_=d8[:, c0:c1], func=AF.Exp),
                   waits=[WV(sv, sv_stats),
                          WV(sv, state.get(("pu_free", g), 0))] + pg,
                   incs=[(ss, 1)])
                state["resc_ss"] = ss.n
                for s in range(G):
                    j = g * G + s
                    bank = pb[kb + s]
                    op("scalar", lambda e, s=s, j=j, bank=bank, pu=pu:
                       e.activation(out=pu[:, s * L:(s + 1) * L], in_=bank,
                                    func=AF.Exp, bias=ns8[:, j:j + 1]),
                       incs=[(ss, 1)])
                    bank_free[kb + s] = (0, ss.n)
                state[("ss_exp", g)] = ss.n
            k_stats.append(stats_thunk)

            def pools_thunk(g=g, kb=kb, ti=ti, mid=None):
                pu = pu4[g]
                c0, c1 = g * G, g * G + G
                pA3 = pAb.rearrange("p (s l) -> p s l", s=G)
                pV3 = pVb.rearrange("p (s l) -> p s l", s=G)
                vsl = v_bf[:, c0 * L:c1 * L]
                srsl = sr[:, c0 * L:c1 * L]
                rden_prev = state.get("rden_ss", 0)
                op("vector", lambda e, c0=c0, c1=c1: e.tensor_tensor(
                    out=a0c[:, c0:c1], in0=carA[:, c0:c1], in1=resc8[:, c0:c1],
                    op=ALU.mult), waits=[WV(ss, state[("ss_exp", g)])])
                op("vector", lambda e, c0=c0, c1=c1: e.tensor_tensor(
                    out=av0c[:, c0:c1], in0=carAv[:, c0:c1],
                    in1=resc8[:, c0:c1], op=ALU.mult), incs=[(sv, 1)])
                sv_a0 = sv.n
                op("vector", lambda e, pu=pu, vsl=vsl: e.tensor_tensor(
                    out=pwb, in0=pu, in1=vsl, op=ALU.mult),
                   waits=[WV(ss, rden_prev)])
                for s in range(G):
                    j = g * G + s
                    op("vector", lambda e, s=s, j=j, pu=pu:
                       e.tensor_tensor_scan(
                           out=pAb[:, s * L:(s + 1) * L],
                           data0=dcol[:, j:j + 1].broadcast_to((P, L)),
                           data1=pu[:, s * L:(s + 1) * L],
                           initial=a0c[:, j:j + 1], op0=ALU.mult, op1=ALU.add),
                       waits=[WV(sv, sv_a0)] + (
                           [WV(ss, ss_dcol)] if (ti == 0 and g == 0 and s == 0)
                           else []))
                for s in range(G):
                    j = g * G + s
                    op("vector", lambda e, s=s, j=j:
                       e.tensor_tensor_scan(
                           out=pVb[:, s * L:(s + 1) * L],
                           data0=dcol[:, j:j + 1].broadcast_to((P, L)),
                           data1=pwb[:, s * L:(s + 1) * L],
                           initial=av0c[:, j:j + 1], op0=ALU.mult, op1=ALU.add))
                op("vector", lambda e, c0=c0, c1=c1, pA3=pA3: e.tensor_copy(
                    out=carA[:, c0:c1], in_=pA3[:, :, L - 1]))
                op("vector", lambda e, c0=c0, c1=c1, pV3=pV3: e.tensor_copy(
                    out=carAv[:, c0:c1], in_=pV3[:, :, L - 1]))
                for s in range(G):
                    j = g * G + s
                    op("vector", lambda e, s=s, j=j: e.scalar_tensor_tensor(
                        out=pnum[:, s * L + 1:(s + 1) * L],
                        in0=pwb[:, s * L + 1:(s + 1) * L],
                        scalar=tfe8[:, j:j + 1],
                        in1=pVb[:, s * L:(s + 1) * L - 1],
                        op0=ALU.mult, op1=ALU.add))
                    op("vector", lambda e, s=s, j=j: e.scalar_tensor_tensor(
                        out=pnum[:, s * L:s * L + 1],
                        in0=pwb[:, s * L:s * L + 1], scalar=tfe8[:, j:j + 1],
                        in1=av0c[:, j:j + 1], op0=ALU.mult, op1=ALU.add))
                for s in range(G):
                    j = g * G + s
                    op("vector", lambda e, s=s, j=j: e.scalar_tensor_tensor(
                        out=pwb[:, s * L + 1:(s + 1) * L],
                        in0=pu[:, s * L + 1:(s + 1) * L],
                        scalar=tfe8[:, j:j + 1],
                        in1=pAb[:, s * L:(s + 1) * L - 1],
                        op0=ALU.mult, op1=ALU.add))
                    op("vector", lambda e, s=s, j=j: e.scalar_tensor_tensor(
                        out=pwb[:, s * L:s * L + 1],
                        in0=pu[:, s * L:s * L + 1], scalar=tfe8[:, j:j + 1],
                        in1=a0c[:, j:j + 1], op0=ALU.mult, op1=ALU.add),
                       incs=([(sv, 1)] if s == G - 1 else []))
                sv_den = sv.n
                op("scalar", lambda e: e.activation(
                    out=pAb, in_=pwb, func=AF.Ln), waits=[WV(sv, sv_den)],
                   incs=[(ss, 1)])
                op("scalar", lambda e: e.activation(
                    out=pwb, in_=pAb, func=AF.Exp, scale=-1.0),
                   incs=[(ss, 1)])
                state["rden_ss"] = ss.n
                if mid is not None:
                    mid()
                op("vector", lambda e: e.tensor_mul(
                    out=pnum, in0=pnum, in1=pwb),
                   waits=[WV(ss, state["rden_ss"])])
                op("vector", lambda e, c0=c0, c1=c1, srsl=srsl:
                   e.tensor_tensor(out=rwkv[:, c0 * L:c1 * L], in0=pnum,
                                   in1=srsl, op=ALU.mult),
                   incs=[(sv, 1)])
                state[("pu_free", g)] = sv.n
            k_pools.append(pools_thunk)

        k_mm_emit(0)
        k_mm_emit(1)
        k_stats[0]()
        k_mm_emit(2)
        k_stats[1]()
        k_mm_emit(3)
        if ti > 0:
            dnu[2] = load_slot(0, dn_loader(2))
        k_stats[2]()
        k_stats[3]()
        if pipeline_A and ti + 1 < NT:
            a1_next = emit_phaseA1(ti + 1)
        emit_hn(1)
        k_pools[0](mid=lambda: emit_hn(2))
        k_pools[1](mid=lambda: emit_hn(3))
        if pipeline_A and ti + 1 < NT:
            a2i_next = emit_phaseA2a()
        k_pools[2]()
        k_pools[3]()
        sv_wkv = sv.n
        if ti == 0:
            dnu[0] = load_slot(1, dn_loader(0), extra_waits=dn_extra())
            dnu[1] = load_slot(2, dn_loader(1), extra_waits=dn_extra())
            dnu[2] = load_slot(0, dn_loader(2), extra_waits=dn_extra())

        for q in range(NQ):
            if q == 1:
                dnu[3] = load_slot(1, dn_loader(3), extra_waits=dn_extra())
            if q == 2:
                ou = load_attn(2, WOT, WOB, "o")
            dsl, dw = dnu[q]
            for j in [4, 5, 6, 7, 0, 1, 2, 3]:
                bank = pb[j]
                for fi in range(CB):
                    fblk = q * CB + fi
                    wts = []
                    if q == 0 and fi == 0:
                        wts = [WV(sv, sv_hn_q[q]), WV(s_w[dsl], dw),
                               WV(sv, bank_free[j][0]), WV(ss, bank_free[j][1])]
                    elif j == 4 and fi == 0:
                        wts = [WV(sv, sv_hn_q[q]), WV(s_w[dsl], dw)]
                    op("tensor", lambda e, j=j, fi=fi, bank=bank, dsl=dsl,
                       fblk=fblk, q=q: e.matmul(
                           out=bank,
                           lhsT=slots[dsl][:, fi * 1024 + j * P:fi * 1024 + (j + 1) * P],
                           rhs=h[:, fblk * L:(fblk + 1) * L],
                           start=(q == 0 and fi == 0), stop=False),
                       waits=wts,
                       incs=([(st, 1)] if (j == 3 and fi == CB - 1)
                             else []))
            slot_free_at[dsl] = st.n
        osl, ow = ou
        jorder = [6, 5, 4, 3, 2, 7, 0, 1]
        for cb in range(CB):
            cw = [WV(sv, state[("pu_free", cb // 2)])]
            if cb == 0:
                cw.append(WV(s_w[osl], ow))
            for ji, j in enumerate(jorder):
                op("tensor", lambda e, j=j, cb=cb, osl=osl: e.matmul(
                    out=pb[j],
                    lhsT=slots[osl][:, cb * 1024 + j * P:cb * 1024 + (j + 1) * P],
                    rhs=rwkv[:, cb * L:(cb + 1) * L],
                    start=False, stop=(cb == CB - 1)),
                   waits=(cw if ji == 0 else []),
                   incs=([(st, 1)] if cb == CB - 1 else []))
                if cb == CB - 1:
                    state[("st_oj", j)] = st.n
        slot_free_at[osl] = st.n

        def xres_load(ji):
            j = jorder[ji]
            par = ji % 4
            xw = [WV(sv, state["xrso"][par])]
            if ti == 0 and ji < 2:
                xw.append(WV(s_pw, prep.get(("dn", "wdone"), 0)))
            op("sync", lambda e, j=j, par=par, ti=ti: e.dma_start(
                out=xres[par], in_=XFP4[:, ti, j, :]),
               waits=xw, incs=[(s_xr[par], 16)])
            state[("xr", j)] = s_xr[par].n
        for ji in range(4):
            xres_load(ji)

        def evac_one(ji, j):
            par = ji % 4
            ob = ji % 4
            ew = [WV(st, state[("st_oj", j)]),
                  WV(s_xr[par], state[("xr", j)]),
                  WV(s_o[ob], state["oso"][ob])]
            op("vector", lambda e, j=j, par=par, ob=ob: e.tensor_add(
                out=out_sb[ob], in0=pb[j], in1=xres[par]),
               waits=ew, incs=[(sv, 1)])
            sv_ev = sv.n
            state["xrso"][par] = sv_ev
            op("sync", lambda e, j=j, ob=ob, t0=t0: e.dma_start(
                out=OUT[j * P:(j + 1) * P, t0:t0 + L], in_=out_sb[ob]),
               waits=[WV(sv, sv_ev)], incs=[(s_o[ob], 16)])
            state["oso"][ob] = s_o[ob].n
            bank_free[j] = (sv_ev, bank_free[j][1])
            if ji + 4 < CB:
                xres_load(ji + 4)

        for ji in range(3):
            evac_one(ji, jorder[ji])
        if pipeline_A and ti + 1 < NT:
            emit_phaseA2b(ti + 1, a1_next, *a2i_next)
        for ji in range(3, CB):
            evac_one(ji, jorder[ji])

    for s4 in s_o:
        op("sync", lambda e, s4=s4: e.wait_ge(s4.h, s4.n))

    block = ctx.enter_context(nc.Block())

    def mk_body(eng_name):
        def body(eng):
            for waits, incs, fn in ops[eng_name]:
                for s, v in waits:
                    eng.wait_ge(s.h, v)
                instr = fn(eng)
                for s, k in incs:
                    instr.then_inc(s.h, k)
        return body

    block.sync(mk_body("sync"))
    block.gpsimd(mk_body("gpsimd"))
    block.vector(mk_body("vector"))
    block.scalar(mk_body("scalar"))
    block.tensor(mk_body("tensor"))
    ctx.close()
    return nc


def _host_stage(inputs, T, L=512):
    x = np.asarray(inputs["L0_attn_x"], np.float32)
    B = x.shape[0]
    v8 = lambda a: np.ascontiguousarray(
        np.asarray(a, np.float32).reshape(CB, P).T)

    def pmaj(w):
        wt = np.asarray(w, np.float32).T
        cin, co = wt.shape
        return np.ascontiguousarray(
            wt.reshape(cin // P, P, co).transpose(1, 0, 2).reshape(P, -1))

    def pmaj_up(w):
        wt = np.asarray(w, np.float32).T
        return np.ascontiguousarray(
            wt.reshape(CB, P, NQ, FQ).transpose(1, 2, 0, 3).reshape(P, -1))

    common = dict(
        WKT=pmaj(inputs["L0_attn_w_k"]),
        WVT=pmaj(inputs["L0_attn_w_v"]),
        WRT=pmaj(inputs["L0_attn_w_r"]),
        WOT=pmaj(inputs["L0_attn_w_o"]),
        WUT=pmaj_up(inputs["L0_ffn_w_up"]),
        WDT=pmaj(inputs["L0_ffn_w_down"]),
        TD8=v8(inputs["L0_attn_time_decay"]),
        TF8=v8(inputs["L0_attn_time_first"]),
        MK8=v8(inputs["L0_attn_time_mix_key"]),
        MV8=v8(inputs["L0_attn_time_mix_value"]),
        MR8=v8(inputs["L0_attn_time_mix_receptance"]),
    )
    in_maps = []
    for b in range(B):
        m = dict(common)
        xc = np.ascontiguousarray(x[b, :T].T)
        m["XF"] = np.ascontiguousarray(
            xc.reshape(CB, P, T // L, L).transpose(1, 2, 0, 3).reshape(P, -1))
        in_maps.append(m)
    return in_maps


def kernel(**inputs):
    T = 4096
    in_maps = _host_stage(inputs, T)
    nc = build_program(T=T, L=512)
    res = run_bass_kernel_spmd(nc, in_maps, core_ids=list(range(len(in_maps))))
    outs = [np.ascontiguousarray(np.asarray(r["OUT"]).T) for r in res.results]
    return np.stack(outs, axis=0)


# version 23 (final)
# speedup vs baseline: 1.0452x; 1.0402x over previous
import numpy as np
import ml_dtypes
import concourse.bass as bass
import concourse.mybir as mybir
from concourse.bass_utils import run_bass_kernel_spmd
from contextlib import ExitStack

F32 = mybir.dt.float32
BF16 = mybir.dt.bfloat16
AF = mybir.ActivationFunctionType
ALU = mybir.AluOpType
AXX = mybir.AxisListType.X

P, C, DFF = 128, 1024, 4096
CB, FB = C // P, DFF // P
NG, G = 4, 2
NQ = 4
FQ = DFF // NQ


class _Sem:
    __slots__ = ("h", "n")
    def __init__(self, h):
        self.h = h
        self.n = 0


def build_program(T=4096, L=512, pipeline_A=True, s_rden=True):
    NT = T // L
    nc = bass.Bass("TRN2", target_bir_lowering=False,
                   detect_race_conditions=False)

    XF = nc.dram_tensor("XF", [P, T * CB], F32, kind="ExternalInput").ap()
    WKT = nc.dram_tensor("WKT", [P, CB * C], F32, kind="ExternalInput").ap()
    WVT = nc.dram_tensor("WVT", [P, CB * C], F32, kind="ExternalInput").ap()
    WRT = nc.dram_tensor("WRT", [P, CB * C], F32, kind="ExternalInput").ap()
    WOT = nc.dram_tensor("WOT", [P, CB * C], F32, kind="ExternalInput").ap()
    WUT = nc.dram_tensor("WUT", [P, CB * DFF], F32, kind="ExternalInput").ap()
    WDT = nc.dram_tensor("WDT", [P, CB * DFF], F32, kind="ExternalInput").ap()
    TD8 = nc.dram_tensor("TD8", [P, CB], F32, kind="ExternalInput").ap()
    TF8 = nc.dram_tensor("TF8", [P, CB], F32, kind="ExternalInput").ap()
    MK8 = nc.dram_tensor("MK8", [P, CB], F32, kind="ExternalInput").ap()
    MV8 = nc.dram_tensor("MV8", [P, CB], F32, kind="ExternalInput").ap()
    MR8 = nc.dram_tensor("MR8", [P, CB], F32, kind="ExternalInput").ap()
    OUT = nc.dram_tensor("OUT", [C, T], F32, kind="ExternalOutput").ap()
    UPB = nc.dram_tensor("UPB", [P, CB * DFF], BF16, kind="Internal").ap()
    DNB = nc.dram_tensor("DNB", [P, CB * DFF], BF16, kind="Internal").ap()
    WKB = nc.dram_tensor("WKB", [P, CB * C], BF16, kind="Internal").ap()
    WVB = nc.dram_tensor("WVB", [P, CB * C], BF16, kind="Internal").ap()
    WRB = nc.dram_tensor("WRB", [P, CB * C], BF16, kind="Internal").ap()
    WOB = nc.dram_tensor("WOB", [P, CB * C], BF16, kind="Internal").ap()

    XFP4 = XF.rearrange("p (n c l) -> p n c l", n=NT, c=CB)

    ctx = ExitStack()
    sb = lambda n, s, d=F32: ctx.enter_context(nc.sbuf_tensor(n, s, d)).ap()
    ps = lambda n, s: ctx.enter_context(nc.psum_tensor(n, s, F32)).ap()
    mksem = lambda n: _Sem(ctx.enter_context(nc.semaphore(n)))

    ones_f = sb("ones_f", [P, P])
    ones_b = sb("ones_b", [P, P], BF16)
    td8 = sb("td8", [P, CB]); tf8 = sb("tf8", [P, CB]); dcol = sb("dcol", [P, CB])
    mk8 = sb("mk8", [P, CB]); mv8 = sb("mv8", [P, CB]); mr8 = sb("mr8", [P, CB])
    omk8 = sb("omk8", [P, CB]); omv8 = sb("omv8", [P, CB]); omr8 = sb("omr8", [P, CB])
    sprev = sb("sprev", [P, CB]); snew = sb("snew", [P, CB])
    m8 = sb("m8", [P, CB]); d8 = sb("d8", [P, CB]); ns8 = sb("ns8", [P, CB])
    tfe8 = sb("tfe8", [P, CB]); resc8 = sb("resc8", [P, CB])
    carA = sb("carA", [P, CB]); carAv = sb("carAv", [P, CB])
    a0c = sb("a0c", [P, CB]); av0c = sb("av0c", [P, CB])
    xprev = sb("xprev", [P, CB], BF16)
    thrU = sb("thrU", [P, 1]); nthrU = sb("nthrU", [P, 1])
    ln05 = sb("ln05", [P, 1])
    thrD = sb("thrD", [P, 1]); nthrD = sb("nthrD", [P, 1])
    CEL = 2 * L
    NCH = C * DFF // (P * CEL)
    accW = sb("accW", [P, 2 * NCH]); accT = sb("accT", [P, 1])

    x_sb = sb("x_sb", [P, CB * L])
    xn = sb("xn", [P, CB * L], BF16)
    xmA = sb("xmA", [P, CB * L], BF16)
    xmB = sb("xmB", [P, CB * L], BF16)
    v_bf = sb("v_bf", [P, CB * L], BF16)
    sr = sb("sr", [P, CB * L], BF16)
    rwkv = sb("rwkv", [P, CB * L], BF16)
    h = sb("h", [P, FB * L], BF16)
    pu4 = [sb(f"pu{i}", [P, G * L]) for i in range(NG)]
    pwb = sb("pwb", [P, G * L])
    pAb = sb("pAb", [P, G * L])
    pVb = sb("pVb", [P, G * L])
    pnum = sb("pnum", [P, G * L])
    mean = sb("mean", [P, L])
    rxn05 = sb("rxn05", [P, L]); rh05 = sb("rh05", [P, L])
    xsq = [sb(f"xsq{i}", [P, L]) for i in range(2)]
    out_sb = [sb(f"outsb{i}", [P, L]) for i in range(4)]
    xres = [sb(f"xres{i}", [P, L]) for i in range(4)]
    hsq = [sb(f"hsq{i}", [P, L], BF16) for i in range(2)]
    slots = [sb(f"slot{i}", [P, CB * 1024], BF16) for i in range(3)]

    pb_full = [ps(f"pb{i}", [P, 512]) for i in range(8)]
    pb = [t[:, 0:L] for t in pb_full]

    x3 = x_sb.rearrange("p (cb l) -> p cb l", cb=CB)
    xn3 = xn.rearrange("p (cb l) -> p cb l", cb=CB)
    xmA3 = xmA.rearrange("p (cb l) -> p cb l", cb=CB)
    xmB3 = xmB.rearrange("p (cb l) -> p cb l", cb=CB)

    mean_b = mean.unsqueeze(1).broadcast_to((P, CB, L))
    rstd_b = xsq[1].unsqueeze(1).broadcast_to((P, CB, L))
    rxn_b = rxn05.unsqueeze(1).broadcast_to((P, CB, L))

    s_ld = mksem("s_ld")
    s_x = mksem("s_x")
    s_xr = [mksem(f"s_xr{i}") for i in range(4)]
    s_o = [mksem(f"s_o{i}") for i in range(4)]
    s_w = [mksem(f"s_w{i}") for i in range(3)]
    s_p = [mksem(f"s_p{i}") for i in range(8)]
    s_pw = mksem("s_pw")
    st = mksem("st")
    sv = mksem("sv")
    ss = mksem("ss")
    sg = mksem("sg")

    ops = {e: [] for e in ("sync", "gpsimd", "vector", "scalar", "tensor")}

    def op(eng, fn, waits=(), incs=()):
        rec = []
        for s, k in incs:
            s.n += k
            rec.append((s, k))
        ops[eng].append(([(s, v) for s, v in waits if v > 0], rec, fn))

    def W(s):
        return (s, s.n)

    def WV(s, v):
        return (s, v)

    op("sync", lambda e: e.dma_start(out=td8, in_=TD8), incs=[(s_ld, 16)])
    op("sync", lambda e: e.dma_start(out=tf8, in_=TF8), incs=[(s_ld, 16)])
    op("sync", lambda e: e.dma_start(out=mk8, in_=MK8), incs=[(s_ld, 16)])
    op("sync", lambda e: e.dma_start(out=mv8, in_=MV8), incs=[(s_ld, 16)])
    op("sync", lambda e: e.dma_start(out=mr8, in_=MR8), incs=[(s_ld, 16)])

    op("vector", lambda e: e.memset(ones_f, 1.0))
    op("vector", lambda e: e.memset(ones_b, 1.0))
    op("vector", lambda e: e.memset(sprev, -1.0e30))
    op("vector", lambda e: e.memset(carA, 0.0))
    op("vector", lambda e: e.memset(carAv, 0.0))
    op("vector", lambda e: e.memset(ln05, -0.6931471805599453))
    op("vector", lambda e: e.memset(xprev, 0.0), incs=[(sv, 1)])
    sv_setup = sv.n
    op("vector", lambda e: e.tensor_scalar(out=omk8, in0=mk8, scalar1=-1.0,
                                           scalar2=1.0, op0=ALU.mult, op1=ALU.add),
       waits=[W(s_ld)])
    op("vector", lambda e: e.tensor_scalar(out=omv8, in0=mv8, scalar1=-1.0,
                                           scalar2=1.0, op0=ALU.mult, op1=ALU.add))
    op("vector", lambda e: e.tensor_scalar(out=omr8, in0=mr8, scalar1=-1.0,
                                           scalar2=1.0, op0=ALU.mult, op1=ALU.add),
       incs=[(sv, 1)])
    sv_omix = sv.n
    op("scalar", lambda e: e.activation(out=tfe8, in_=tf8, func=AF.Exp),
       waits=[W(s_ld)], incs=[(ss, 1)])
    op("scalar", lambda e: e.activation(out=dcol, in_=td8, func=AF.Exp),
       incs=[(ss, 1)])
    op("scalar", lambda e: e.activation(out=dcol, in_=dcol, func=AF.Exp, scale=-1.0),
       waits=[W(ss)], incs=[(ss, 1)])
    ss_dcol = ss.n

    slot_free_at = [0, 0, 0]
    slot_pw_gate = [0, 0, 0]

    def load_slot(sl, fn_builder, extra_waits=()):
        waits = [WV(st, slot_free_at[sl])] + list(extra_waits)
        if slot_pw_gate[sl]:
            waits.append(WV(s_pw, slot_pw_gate[sl]))
            slot_pw_gate[sl] = 0
        op("gpsimd", fn_builder(slots[sl]), waits=waits, incs=[(s_w[sl], 16)])
        return sl, s_w[sl].n

    def attn_w_loader(src):
        def b(slot):
            return lambda e: e.dma_start(out=slot, in_=src)
        return b

    def load_attn(sl_pref, wf32, wbf, key, extra_waits=()):
        if key not in attn_bf:
            sl, cnt = load_slot(sl_pref, attn_w_loader(wf32),
                                extra_waits=extra_waits)
            op("gpsimd", lambda e, sl=sl, wbf=wbf: e.dma_start(
                out=wbf, in_=slots[sl]), incs=[(s_pw, 16)])
            attn_bf[key] = s_pw.n
            slot_pw_gate[sl] = s_pw.n
            return sl, cnt
        ex = [WV(s_pw, attn_bf[key])] + list(extra_waits)
        return load_slot(sl_pref, attn_w_loader(wbf), extra_waits=ex)

    attn_bf = {}

    def up_loader(q):
        def b(slot):
            return lambda e: e.dma_start(
                out=slot, in_=UPB[:, q * CB * FQ:(q + 1) * CB * FQ])
        return b

    def dn_loader(q):
        def b(slot):
            return lambda e: e.dma_start(
                out=slot, in_=DNB[:, q * CB * C:(q + 1) * CB * C])
        return b

    prep = {}

    h_f32 = h.bitcast(F32)
    ubufs = [h_f32[:, i * CEL:(i + 1) * CEL] for i in range(8)] + pu4
    ubuf_t = [h[:, 2 * i * CEL:2 * i * CEL + CEL] for i in range(8)] + \
             [t.bitcast(BF16)[:, 0:CEL] for t in pu4]
    xmA_f = xmA.bitcast(F32)
    vbf_f = v_bf.bitcast(F32)
    dbufs = [pwb, pAb, pVb, pnum,
             xmA_f[:, 0:CEL], xmA_f[:, CEL:2 * CEL],
             vbf_f[:, 0:CEL], vbf_f[:, CEL:2 * CEL]]
    dbuf_t = [t.bitcast(BF16)[:, 0:CEL] for t in (pwb, pAb, pVb, pnum)] + \
             [xmA[:, 0:CEL], xmA[:, 2 * CEL:3 * CEL],
              v_bf[:, 0:CEL], v_bf[:, 2 * CEL:3 * CEL]]
    scr_u = [sr[:, i * CEL:(i + 1) * CEL] for i in range(4)]
    scr_d = [rwkv[:, i * CEL:(i + 1) * CEL] for i in range(4)]

    def _pp(which):
        if which == "up":
            return WUT, UPB, ubufs, scr_u, thrU, nthrU, 0, ubuf_t
        return WDT, DNB, dbufs, scr_d, thrD, nthrD, NCH, dbuf_t

    def _prep_load(which, ci, ph):
        wsrc, _, bufs, _, _, _, _, _ = _pp(which)
        if ci >= NCH:
            return
        b = ci % len(bufs)
        si = (0 if which == "up" else 4) + ci % 4
        waits = []
        lu = prep.get((which, "lastuse", b), None)
        if lu is not None:
            waits.append(lu)
        if ph == 0:
            if ci >= 4:
                waits.append(WV(ss, prep[(which, "ause", ci - 4)]))
        else:
            waits.append(WV(ss, prep[(which, "tuse", ci - 4)] if ci >= 4
                            else prep[(which, "ause", NCH - 4 + ci)]))
        op("gpsimd", lambda e, ci=ci, b=b, bufs=bufs, wsrc=wsrc:
           e.dma_start(out=bufs[b], in_=wsrc[:, ci * CEL:(ci + 1) * CEL]),
           waits=waits, incs=[(s_p[si], 16)])
        prep[(which, "ld", ci)] = (si, s_p[si].n)

    def emit_prep_alpha_chunk(which, ci):
        _, _, bufs, _, _, _, acol, _ = _pp(which)
        nb = len(bufs)
        if ci == 0:
            for c2 in range(min(4, NCH)):
                _prep_load(which, c2, 0)
        b = ci % nb
        si, cnt = prep[(which, "ld", ci)]
        op("scalar", lambda e, b=b, ci=ci, bufs=bufs, acol=acol:
           e.activation(out=bufs[b], in_=bufs[b], func=AF.Abs,
                        accum_out=accW[:, acol + ci:acol + ci + 1]),
           waits=[WV(s_p[si], cnt)], incs=[(ss, 1)])
        prep[(which, "ause", ci)] = ss.n
        prep[(which, "lastuse", b)] = WV(ss, ss.n)
        _prep_load(which, ci + 4, 0)

    def emit_prep_alpha_fin(which):
        acol = _pp(which)[6]
        op("vector", lambda e, acol=acol:
           e.tensor_reduce(out=accT, in_=accW[:, acol:acol + NCH],
                           axis=AXX, op=ALU.add),
           waits=[W(ss)], incs=[(sv, 1)])
        sv_acc = sv.n
        op("tensor", lambda e: e.matmul(out=pb[5][:, 0:1], lhsT=ones_f,
                                        rhs=accT, start=True, stop=True),
           waits=[WV(sv, sv_acc), WV(sv, sv_setup),
                  WV(sv, bank_free[5][0]), WV(ss, bank_free[5][1])],
           incs=[(st, 1)])
        st_mm = st.n
        thr = thrU if which == "up" else thrD
        nthr = nthrU if which == "up" else nthrD
        sc = 0.5 / float(C * DFF)
        op("scalar", lambda e, thr=thr, sc=sc:
           e.mul(out=thr, in_=pb[5][:, 0:1], mul=sc),
           waits=[WV(st, st_mm)], incs=[(ss, 1)])
        op("scalar", lambda e, nthr=nthr, sc=sc:
           e.mul(out=nthr, in_=pb[5][:, 0:1], mul=-sc), incs=[(ss, 1)])
        prep[(which, "thr")] = ss.n
        bank_free[5] = (0, ss.n)

    def emit_prep_tern_chunk(which, ci, oi):
        _, wdst, bufs, scr, thr, nthr, _, tbufs = _pp(which)
        nb = len(bufs)
        if oi == 0:
            for c2 in range(min(4, NCH)):
                _prep_load(which, c2, 1)
        b = ci % nb
        s1 = scr[2 * (oi % 2)]
        s2 = scr[2 * (oi % 2) + 1]
        si, cnt = prep[(which, "ld", ci)]
        wts = [WV(ss, prep[(which, "thr")]),
               WV(s_p[si], cnt),
               WV(sv, prep.get((which, "tadd", oi - 2), 0))]
        op("scalar", lambda e, b=b, s1=s1, thr=thr, bufs=bufs:
           e.sign(out=s1, in_=bufs[b], bias=thr), waits=wts)
        op("scalar", lambda e, b=b, s2=s2, nthr=nthr, bufs=bufs:
           e.sign(out=s2, in_=bufs[b], bias=nthr), incs=[(ss, 1)])
        prep[(which, "tuse", ci)] = ss.n
        op("vector", lambda e, b=b, s1=s1, s2=s2, tbufs=tbufs:
           e.tensor_add(out=tbufs[b], in0=s1, in1=s2),
           waits=[W(ss)], incs=[(sv, 1)])
        prep[(which, "tadd", oi)] = sv.n
        op("gpsimd", lambda e, ci=ci, b=b, wdst=wdst, tbufs=tbufs:
           e.dma_start(out=wdst[:, ci * CEL:(ci + 1) * CEL], in_=tbufs[b]),
           waits=[WV(sv, sv.n)], incs=[(s_pw, 16)])
        prep[(which, "lastuse", b)] = WV(s_pw, s_pw.n)
        _prep_load(which, ci + 4, 1)

    def _done(which):
        prep[(which, "wdone")] = s_pw.n

    prep_q = []
    for ci in range(NCH):
        prep_q.append(lambda ci=ci: emit_prep_alpha_chunk("up", ci))
        prep_q.append(lambda ci=ci: emit_prep_alpha_chunk("dn", ci))
    prep_q.append(lambda: emit_prep_alpha_fin("up"))
    prep_q.append(lambda: emit_prep_alpha_fin("dn"))
    prep_q += [lambda ci=ci: emit_prep_tern_chunk("up", ci, ci)
               for ci in range(NCH)]
    prep_q.append(lambda: _done("up"))
    prep_q += [lambda ci=ci: emit_prep_tern_chunk("dn", ci, ci)
               for ci in range(NCH)]
    prep_q.append(lambda: _done("dn"))
    dn_q = []

    def drain(q, n):
        for _ in range(min(n, len(q))):
            q.pop(0)()

    bank_free = [(0, 0) for _ in range(8)]
    state = {"wkv_sv": {}, "hsq_st": {}, "xsq_st": {}, "oso": [0, 0, 0, 0],
             "xA": None, "phA": {}, "st_up": 0,
             "st_r": 0, "st_v": 0, "st_k": 0, "xrso": [0, 0, 0, 0]}

    def emit_phaseA1(ti):
        t0 = ti * L
        xw = []
        if state["xA"] is not None:
            a = state["xA"]
            xw = [WV(st, a["st"]), WV(ss, a["ss"]), WV(sv, a["sv"]),
                  WV(s_pw, state.pop("xA_pw", 0))]
        op("sync", lambda e, ti=ti: e.dma_start(out=x3, in_=XFP4[:, ti]),
           waits=xw, incs=[(s_x, 16)])
        sx_here = s_x.n
        for cb in range(CB):
            wts = []
            if cb == 0:
                wts = [WV(s_x, sx_here), WV(sv, sv_setup),
                       WV(sv, bank_free[7][0]), WV(ss, bank_free[7][1])]
            op("tensor", lambda e, cb=cb: e.matmul(
                out=pb[7], lhsT=ones_f, rhs=x3[:, cb, :],
                start=(cb == 0), stop=(cb == CB - 1)),
               waits=wts, incs=([(st, 1)] if cb == CB - 1 else []))
        st_sx = st.n
        op("vector", lambda e: e.tensor_scalar_mul(out=mean, in0=pb[7],
                                                   scalar1=1.0 / C),
           waits=[WV(st, st_sx)], incs=[(sv, 1)])
        sv_mean = sv.n
        for cb in range(CB):
            par = cb % 2
            op("scalar", lambda e, cb=cb, par=par: e.activation(
                out=xsq[par], in_=x3[:, cb, :], func=AF.Square),
               waits=[WV(s_x, sx_here), WV(st, state["xsq_st"].get(par, 0))],
               incs=[(ss, 1)])
            ss_xsq = ss.n
            wts = [WV(ss, ss_xsq)]
            if cb == 0:
                wts += [WV(sv, sv_mean)]
            op("tensor", lambda e, par=par, cb=cb: e.matmul(
                out=pb[7], lhsT=ones_f, rhs=xsq[par],
                start=(cb == 0), stop=(cb == CB - 1)),
               waits=wts, incs=[(st, 1)])
            state["xsq_st"][par] = st.n
        st_sx2 = st.n
        ss_sq = ss.n
        op("vector", lambda e: e.tensor_scalar_mul(out=xsq[0], in0=pb[7],
                                                   scalar1=1.0 / C),
           waits=[WV(st, st_sx2)], incs=[(sv, 1)])
        sv_b67 = sv.n
        bank_free[7] = (sv_b67, 0)
        return dict(sx=sx_here, sv_b67=sv_b67, st_sx2=st_sx2, ss_sq=ss_sq)

    def emit_phaseA2(ti, a1):
        op("vector", lambda e: e.tensor_mul(out=xsq[1], in0=mean, in1=mean))
        op("vector", lambda e: e.tensor_sub(out=xsq[0], in0=xsq[0], in1=xsq[1]))
        op("vector", lambda e: e.tensor_scalar_add(out=xsq[1], in0=xsq[0],
                                                   scalar1=1.0e-5),
           incs=[(sv, 1)])
        sv_vpe = sv.n
        op("scalar", lambda e: e.activation(out=xsq[1], in_=xsq[1], func=AF.Ln),
           waits=[WV(sv, sv_vpe)], incs=[(ss, 1)])
        op("scalar", lambda e: e.activation(out=rh05, in_=xsq[0], func=AF.Ln),
           waits=[WV(sv, state.get("sv_hn", 0))],
           incs=[(ss, 1)])
        ss_ln = ss.n
        op("vector", lambda e: e.tensor_sub(out=rh05, in0=rh05, in1=xsq[1]),
           waits=[WV(ss, ss_ln)], incs=[(sv, 1)])
        op("scalar", lambda e: e.activation(out=rh05, in_=rh05, func=AF.Exp,
                                            scale=0.5), waits=[W(sv)],
           incs=[(ss, 1)])
        op("scalar", lambda e: e.activation(out=xsq[1], in_=xsq[1], func=AF.Exp,
                                            scale=-0.5), incs=[(ss, 1)])
        ss_e = ss.n
        op("vector", lambda e: e.tensor_scalar_add(out=rh05, in0=rh05,
                                                   scalar1=1.0e-8),
           waits=[WV(ss, ss_e)], incs=[(sv, 1)])
        op("scalar", lambda e: e.activation(out=rh05, in_=rh05, func=AF.Ln),
           waits=[W(sv)], incs=[(ss, 1)])
        op("scalar", lambda e: e.activation(out=rxn05, in_=rh05, func=AF.Exp,
                                            scale=-1.0, bias=ln05),
           incs=[(ss, 1)])
        ss_rxn = ss.n
        op("vector", lambda e: e.tensor_sub(out=xmB3, in0=x3, in1=mean_b),
           waits=[WV(st, state["st_v"]), WV(ss, ss_e)], incs=[(sv, 1)])
        op("vector", lambda e: e.tensor_tensor(out=xn3, in0=xmB3, in1=rstd_b,
                                               op=ALU.mult))
        op("vector", lambda e: e.tensor_tensor(out=xmB3, in0=xn3, in1=rxn_b,
                                               op=ALU.mult),
           waits=[WV(ss, ss_rxn)], incs=[(sv, 1)])
        sv_xnn = sv.n
        state["phA"][ti] = dict(sv_xnn=sv_xnn, **a1)
        state["xA"] = dict(st=a1["st_sx2"], ss=a1["ss_sq"], sv=sv_xnn)

    def emit_lerp(mix, omix, dst3, extra_w=()):
        for cb in range(CB):
            wts = list(extra_w) if cb == 0 else []
            op("vector", lambda e, cb=cb, mix=mix, dst3=dst3:
               e.tensor_scalar_mul(out=dst3[:, cb, :], in0=xn3[:, cb, :],
                                   scalar1=mix[:, cb:cb + 1]), waits=wts)
            op("vector", lambda e, cb=cb, omix=omix, dst3=dst3:
               e.scalar_tensor_tensor(
                   out=dst3[:, cb, 1:L], in0=xn3[:, cb, 0:L - 1],
                   scalar=omix[:, cb:cb + 1], in1=dst3[:, cb, 1:L],
                   op0=ALU.mult, op1=ALU.add))
        op("vector", lambda e, omix=omix: e.tensor_tensor(
            out=d8, in0=xprev, in1=omix, op=ALU.mult),
           waits=[WV(ss, state.get("resc_ss", 0))], incs=[(sv, 1)])
        op("vector", lambda e, dst3=dst3: e.tensor_tensor(
            out=dst3[:, :, 0], in0=dst3[:, :, 0], in1=d8, op=ALU.add),
           waits=[W(sv)], incs=[(sv, 1)])
        return sv.n

    for ti in range(NT):
        t0 = ti * L
        if ti == 0 or not pipeline_A:
            a1 = emit_phaseA1(ti)
            emit_phaseA2(ti, a1)
        phA = state["phA"][ti]

        if ti == 0:
            drain(prep_q, len(prep_q))
        sv_lr = emit_lerp(mr8, omr8, xmA3, extra_w=[
            WV(st, state["st_k"]),
            WV(s_pw, prep.get(("dn", "wdone"), 0))])

        up_extra = ([WV(s_pw, prep.get(("up", "wdone"), 0))] if ti == 0 else [])
        upu = [None] * NQ
        upu[0] = load_slot(0, up_loader(0), extra_waits=up_extra)
        upu[1] = load_slot(1, up_loader(1), extra_waits=up_extra)
        upu[2] = load_slot(2, up_loader(2), extra_waits=up_extra)

        ubanks = [5, 4, 2, 3]
        for q in range(NQ):
            if q == 1:
                upu[3] = load_slot(0, up_loader(3), extra_waits=up_extra)
            if q == 2:
                ru = load_attn(1, WRT, WRB, "r")
            if q == 3:
                vu = load_attn(2, WVT, WVB, "v")
            usl, uw = upu[q]

            def emit_sh2(g):
                if g < 0:
                    return
                wts = [WV(ss, state[("sh2ss", g)])]
                if g == 0:
                    wts += [WV(sv, bank_free[6][0]), WV(ss, bank_free[6][1]),
                            WV(sv, sv_setup)]
                op("tensor", lambda e, g=g: e.matmul(
                    out=pb[6], lhsT=ones_b, rhs=hsq[g % 2],
                    start=(g == 0), stop=(g == FB - 1)),
                   waits=wts, incs=[(st, 1)])
                state["hsq_st"][g % 2] = st.n

            for fi in range(FQ // P):
                gblk = q * (FQ // P) + fi
                ub = ubanks[gblk % 4]
                bank = pb[ub]
                for cb in range(CB):
                    wts = []
                    if cb == 0:
                        wts = [WV(sv, phA["sv_xnn"]), WV(s_w[usl], uw),
                               WV(sv, bank_free[ub][0]),
                               WV(ss, bank_free[ub][1])]
                    op("tensor", lambda e, cb=cb, fi=fi, bank=bank, usl=usl:
                       e.matmul(
                           out=bank,
                           lhsT=slots[usl][:, cb * FQ + fi * P:cb * FQ + (fi + 1) * P],
                           rhs=xmB3[:, cb, :], start=(cb == 0), stop=(cb == CB - 1)),
                       waits=wts, incs=([(st, 1)] if cb == CB - 1 else []))
                st_up = st.n
                emit_sh2(gblk - 2)
                op("scalar", lambda e, gblk=gblk, bank=bank: e.activation(
                    out=h[:, gblk * L:(gblk + 1) * L], in_=bank, func=AF.Gelu),
                   waits=[WV(st, st_up)], incs=[(ss, 1)])
                op("scalar", lambda e, gblk=gblk: e.activation(
                    out=hsq[gblk % 2], in_=h[:, gblk * L:(gblk + 1) * L],
                    func=AF.Square),
                   waits=[WV(st, state["hsq_st"].get(gblk % 2, 0))],
                   incs=[(ss, 1)])
                state[("sh2ss", gblk)] = ss.n
                bank_free[ub] = (0, ss.n)
            slot_free_at[usl] = st.n
        emit_sh2(FB - 2)
        emit_sh2(FB - 1)
        ku = load_attn(0, WKT, WKB, "k")
        st_sh2 = st.n
        state["st_up"] = st_sh2

        op("vector", lambda e: e.tensor_scalar_mul(out=rh05, in0=pb[6],
                                                   scalar1=1.0 / DFF),
           waits=[WV(st, st_sh2)], incs=[(sv, 1)])
        bank_free[6] = (sv.n, 0)
        op("scalar", lambda e: e.activation(out=rh05, in_=rh05, func=AF.Ln),
           waits=[W(sv)], incs=[(ss, 1)])
        op("scalar", lambda e: e.activation(out=rh05, in_=rh05, func=AF.Exp,
                                            scale=0.5), incs=[(ss, 1)])
        op("vector", lambda e: e.tensor_scalar_add(out=rh05, in0=rh05,
                                                   scalar1=1.0e-8),
           waits=[W(ss)], incs=[(sv, 1)])
        op("scalar", lambda e: e.activation(out=rh05, in_=rh05, func=AF.Ln),
           waits=[W(sv)], incs=[(ss, 1)])
        op("scalar", lambda e: e.activation(out=rh05, in_=rh05, func=AF.Exp,
                                            scale=-1.0, bias=ln05),
           incs=[(ss, 1)])
        ss_rh = ss.n
        sv_hn_q = [0] * NQ

        def emit_hn(qq):
            for gi in range(FB // NQ):
                gblk = qq * (FB // NQ) + gi
                wts = [WV(ss, ss_rh)] if gblk == 0 else []
                op("vector", lambda e, gblk=gblk: e.tensor_tensor(
                    out=h[:, gblk * L:(gblk + 1) * L],
                    in0=h[:, gblk * L:(gblk + 1) * L], in1=rh05, op=ALU.mult),
                   waits=wts,
                   incs=([(sv, 1)] if gi == FB // NQ - 1 else []))
            sv_hn_q[qq] = sv.n
            if qq == NQ - 1:
                state["sv_hn"] = sv.n

        def proj(slw, src3, sv_src, evac):
            sl, wld = slw
            for j in range(CB):
                bank = pb[j % 4]
                for cb in range(CB):
                    wts = []
                    if cb == 0:
                        wts = [WV(sv, sv_src), WV(s_w[sl], wld),
                               WV(sv, bank_free[j % 4][0]),
                               WV(ss, bank_free[j % 4][1])]
                    op("tensor", lambda e, j=j, cb=cb, bank=bank, sl=sl: e.matmul(
                        out=bank,
                        lhsT=slots[sl][:, cb * 1024 + j * P:cb * 1024 + (j + 1) * P],
                        rhs=src3[:, cb, :], start=(cb == 0), stop=(cb == CB - 1)),
                       waits=wts, incs=([(st, 1)] if cb == CB - 1 else []))
                evac(j, bank, st.n)
            slot_free_at[sl] = st.n
            return st.n

        def evac_r(j, bank, st_c):
            pw = ([WV(s_pw, prep.get(("up", "wdone"), 0))] if j == 0 else [])
            op("scalar", lambda e, j=j, bank=bank: e.activation(
                out=sr[:, j * L:(j + 1) * L], in_=bank, func=AF.Sigmoid),
               waits=[WV(st, st_c)] + pw, incs=[(ss, 1)])
            bank_free[j % 4] = (0, ss.n)

        def evac_v(j, bank, st_c):
            pw = ([WV(s_pw, prep.get(("dn", "wdone"), 0))] if j == 0 else [])
            op("scalar", lambda e, j=j, bank=bank: e.activation(
                out=v_bf[:, j * L:(j + 1) * L], in_=bank, func=AF.Copy),
               waits=[WV(st, st_c)] + pw, incs=[(ss, 1)])
            bank_free[j % 2] = (0, ss.n)

        state["st_r"] = proj(ru, xmA3, sv_lr, evac_r)
        dnu = [None] * NQ

        def dn_extra():
            return ([WV(s_pw, prep.get(("dn", "wdone"), 0))] if ti == 0 else [])
        if ti > 0:
            dnu[0] = load_slot(1, dn_loader(0))

        sv_lv = emit_lerp(mv8, omv8, xmB3, extra_w=[WV(st, state["st_up"])])
        state["st_v"] = proj(vu, xmB3, sv_lv, evac_v)
        if ti > 0:
            dnu[1] = load_slot(2, dn_loader(1))

        sv_lk = emit_lerp(mk8, omk8, xmA3, extra_w=[WV(st, state["st_r"])])
        op("vector", lambda e: e.tensor_copy(out=xprev, in_=xn3[:, :, L - 1]),
           incs=[(sv, 1)])
        emit_hn(0)
        ksl, kw = ku

        def k_mm_emit(g):
            kb = (g % 2) * 2
            for s in range(G):
                j = g * G + s
                bank = pb[kb + s]
                for cb in range(CB):
                    wts = []
                    if cb == 0:
                        wts = [WV(sv, sv_lk), WV(s_w[ksl], kw),
                               WV(sv, bank_free[kb + s][0]),
                               WV(ss, bank_free[kb + s][1])]
                    op("tensor", lambda e, j=j, cb=cb, bank=bank, ksl=ksl: e.matmul(
                        out=bank,
                        lhsT=slots[ksl][:, cb * 1024 + j * P:cb * 1024 + (j + 1) * P],
                        rhs=xmA3[:, cb, :], start=(cb == 0), stop=(cb == CB - 1)),
                       waits=wts, incs=([(st, 1)] if cb == CB - 1 else []))
                state[("st_kj", g, s)] = st.n
            state["st_k"] = st.n
            if g == NG - 1:
                slot_free_at[ksl] = st.n

        k_stats = []
        k_pools = []
        for g in range(NG):
            kb = (g % 2) * 2

            def stats_thunk(g=g, kb=kb, ti=ti):
                stk = (state[("st_kj", g, 0)], state[("st_kj", g, 1)])
                pu = pu4[g]
                c0, c1 = g * G, g * G + G
                op("vector", lambda e, c0=c0, kb=kb: e.tensor_reduce(
                    out=m8[:, c0:c0 + 1], in_=pb[kb], axis=AXX, op=ALU.max),
                   waits=[WV(st, stk[0])])
                op("vector", lambda e, c0=c0, kb=kb: e.tensor_reduce(
                    out=m8[:, c0 + 1:c0 + 2], in_=pb[kb + 1], axis=AXX,
                    op=ALU.max),
                   waits=[WV(st, stk[1])], incs=[(sv, 1)])
                op("vector", lambda e, c0=c0, c1=c1: e.tensor_max(
                    out=snew[:, c0:c1], in0=m8[:, c0:c1], in1=sprev[:, c0:c1]),
                   waits=[W(sv)], incs=[(sv, 1)])
                op("vector", lambda e, c0=c0, c1=c1: e.tensor_sub(
                    out=d8[:, c0:c1], in0=sprev[:, c0:c1], in1=snew[:, c0:c1]),
                   waits=[W(sv)])
                op("vector", lambda e, c0=c0, c1=c1: e.tensor_scalar_mul(
                    out=ns8[:, c0:c1], in0=snew[:, c0:c1], scalar1=-1.0))
                op("vector", lambda e, c0=c0, c1=c1: e.tensor_copy(
                    out=sprev[:, c0:c1], in_=snew[:, c0:c1]), incs=[(sv, 1)])
                sv_stats = sv.n
                pg = []
                if ti == 0:
                    pg = [WV(s_pw, max(prep.get(("up", "wdone"), 0),
                                       prep.get(("dn", "wdone"), 0)))]
                op("scalar", lambda e, c0=c0, c1=c1: e.activation(
                    out=resc8[:, c0:c1], in_=d8[:, c0:c1], func=AF.Exp),
                   waits=[WV(sv, sv_stats),
                          WV(sv, state.get(("pu_free", g), 0))] + pg,
                   incs=[(ss, 1)])
                state["resc_ss"] = ss.n
                for s in range(G):
                    j = g * G + s
                    bank = pb[kb + s]
                    op("scalar", lambda e, s=s, j=j, bank=bank, pu=pu:
                       e.activation(out=pu[:, s * L:(s + 1) * L], in_=bank,
                                    func=AF.Exp, bias=ns8[:, j:j + 1]),
                       incs=[(ss, 1)])
                    bank_free[kb + s] = (0, ss.n)
                state[("ss_exp", g)] = ss.n
            k_stats.append(stats_thunk)

            def pools_thunk(g=g, kb=kb, ti=ti, mid=None):
                pu = pu4[g]
                c0, c1 = g * G, g * G + G
                pA3 = pAb.rearrange("p (s l) -> p s l", s=G)
                pV3 = pVb.rearrange("p (s l) -> p s l", s=G)
                vsl = v_bf[:, c0 * L:c1 * L]
                srsl = sr[:, c0 * L:c1 * L]
                rden_prev = state.get("rden_ss", 0)
                op("vector", lambda e, c0=c0, c1=c1: e.tensor_tensor(
                    out=a0c[:, c0:c1], in0=carA[:, c0:c1], in1=resc8[:, c0:c1],
                    op=ALU.mult), waits=[WV(ss, state[("ss_exp", g)])])
                op("vector", lambda e, c0=c0, c1=c1: e.tensor_tensor(
                    out=av0c[:, c0:c1], in0=carAv[:, c0:c1],
                    in1=resc8[:, c0:c1], op=ALU.mult), incs=[(sv, 1)])
                sv_a0 = sv.n
                op("vector", lambda e, pu=pu, vsl=vsl: e.tensor_tensor(
                    out=pwb, in0=pu, in1=vsl, op=ALU.mult),
                   waits=[WV(ss, rden_prev)])
                for s in range(G):
                    j = g * G + s
                    op("vector", lambda e, s=s, j=j, pu=pu:
                       e.tensor_tensor_scan(
                           out=pAb[:, s * L:(s + 1) * L],
                           data0=dcol[:, j:j + 1].broadcast_to((P, L)),
                           data1=pu[:, s * L:(s + 1) * L],
                           initial=a0c[:, j:j + 1], op0=ALU.mult, op1=ALU.add),
                       waits=[WV(sv, sv_a0)] + (
                           [WV(ss, ss_dcol)] if (ti == 0 and g == 0 and s == 0)
                           else []))
                for s in range(G):
                    j = g * G + s
                    op("vector", lambda e, s=s, j=j:
                       e.tensor_tensor_scan(
                           out=pVb[:, s * L:(s + 1) * L],
                           data0=dcol[:, j:j + 1].broadcast_to((P, L)),
                           data1=pwb[:, s * L:(s + 1) * L],
                           initial=av0c[:, j:j + 1], op0=ALU.mult, op1=ALU.add))
                op("vector", lambda e, c0=c0, c1=c1, pA3=pA3: e.tensor_copy(
                    out=carA[:, c0:c1], in_=pA3[:, :, L - 1]))
                op("vector", lambda e, c0=c0, c1=c1, pV3=pV3: e.tensor_copy(
                    out=carAv[:, c0:c1], in_=pV3[:, :, L - 1]))
                for s in range(G):
                    j = g * G + s
                    op("vector", lambda e, s=s, j=j: e.scalar_tensor_tensor(
                        out=pnum[:, s * L + 1:(s + 1) * L],
                        in0=pwb[:, s * L + 1:(s + 1) * L],
                        scalar=tfe8[:, j:j + 1],
                        in1=pVb[:, s * L:(s + 1) * L - 1],
                        op0=ALU.mult, op1=ALU.add))
                    op("vector", lambda e, s=s, j=j: e.scalar_tensor_tensor(
                        out=pnum[:, s * L:s * L + 1],
                        in0=pwb[:, s * L:s * L + 1], scalar=tfe8[:, j:j + 1],
                        in1=av0c[:, j:j + 1], op0=ALU.mult, op1=ALU.add))
                for s in range(G):
                    j = g * G + s
                    op("vector", lambda e, s=s, j=j: e.scalar_tensor_tensor(
                        out=pwb[:, s * L + 1:(s + 1) * L],
                        in0=pu[:, s * L + 1:(s + 1) * L],
                        scalar=tfe8[:, j:j + 1],
                        in1=pAb[:, s * L:(s + 1) * L - 1],
                        op0=ALU.mult, op1=ALU.add))
                    op("vector", lambda e, s=s, j=j: e.scalar_tensor_tensor(
                        out=pwb[:, s * L:s * L + 1],
                        in0=pu[:, s * L:s * L + 1], scalar=tfe8[:, j:j + 1],
                        in1=a0c[:, j:j + 1], op0=ALU.mult, op1=ALU.add),
                       incs=([(sv, 1)] if s == G - 1 else []))
                sv_den = sv.n
                op("scalar", lambda e: e.activation(
                    out=pAb, in_=pwb, func=AF.Ln), waits=[WV(sv, sv_den)],
                   incs=[(ss, 1)])
                op("scalar", lambda e: e.activation(
                    out=pwb, in_=pAb, func=AF.Exp, scale=-1.0),
                   incs=[(ss, 1)])
                state["rden_ss"] = ss.n
                if mid is not None:
                    mid()
                op("vector", lambda e: e.tensor_mul(
                    out=pnum, in0=pnum, in1=pwb),
                   waits=[WV(ss, state["rden_ss"])])
                op("vector", lambda e, c0=c0, c1=c1, srsl=srsl:
                   e.tensor_tensor(out=rwkv[:, c0 * L:c1 * L], in0=pnum,
                                   in1=srsl, op=ALU.mult),
                   incs=[(sv, 1)])
                state[("pu_free", g)] = sv.n
            k_pools.append(pools_thunk)

        k_mm_emit(0)
        k_mm_emit(1)
        k_stats[0]()
        k_mm_emit(2)
        k_stats[1]()
        k_mm_emit(3)
        if ti > 0:
            dnu[2] = load_slot(0, dn_loader(2))
        k_stats[2]()
        k_stats[3]()
        if pipeline_A and ti + 1 < NT:
            a1_next = emit_phaseA1(ti + 1)
        emit_hn(1)
        k_pools[0](mid=lambda: emit_hn(2))
        k_pools[1](mid=lambda: emit_hn(3))
        k_pools[2]()
        k_pools[3]()
        sv_wkv = sv.n
        if pipeline_A and ti + 1 < NT:
            emit_phaseA2(ti + 1, a1_next)
        if ti == 0:
            dnu[0] = load_slot(1, dn_loader(0), extra_waits=dn_extra())
            dnu[1] = load_slot(2, dn_loader(1), extra_waits=dn_extra())
            dnu[2] = load_slot(0, dn_loader(2), extra_waits=dn_extra())

        for q in range(NQ):
            if q == 1:
                dnu[3] = load_slot(1, dn_loader(3), extra_waits=dn_extra())
            if q == 2:
                ou = load_attn(2, WOT, WOB, "o")
            dsl, dw = dnu[q]
            for j in [4, 5, 6, 7, 0, 1, 2, 3]:
                bank = pb[j]
                for fi in range(CB):
                    fblk = q * CB + fi
                    wts = []
                    if q == 0 and fi == 0:
                        wts = [WV(sv, sv_hn_q[q]), WV(s_w[dsl], dw),
                               WV(sv, bank_free[j][0]), WV(ss, bank_free[j][1])]
                    elif j == 4 and fi == 0:
                        wts = [WV(sv, sv_hn_q[q]), WV(s_w[dsl], dw)]
                    op("tensor", lambda e, j=j, fi=fi, bank=bank, dsl=dsl,
                       fblk=fblk, q=q: e.matmul(
                           out=bank,
                           lhsT=slots[dsl][:, fi * 1024 + j * P:fi * 1024 + (j + 1) * P],
                           rhs=h[:, fblk * L:(fblk + 1) * L],
                           start=(q == 0 and fi == 0), stop=False),
                       waits=wts,
                       incs=([(st, 1)] if (j == 3 and fi == CB - 1)
                             else []))
            slot_free_at[dsl] = st.n
        osl, ow = ou
        jorder = [6, 5, 4, 3, 2, 7, 0, 1]
        for cb in range(CB):
            cw = [WV(sv, state[("pu_free", cb // 2)])]
            if cb == 0:
                cw.append(WV(s_w[osl], ow))
            for ji, j in enumerate(jorder):
                op("tensor", lambda e, j=j, cb=cb, osl=osl: e.matmul(
                    out=pb[j],
                    lhsT=slots[osl][:, cb * 1024 + j * P:cb * 1024 + (j + 1) * P],
                    rhs=rwkv[:, cb * L:(cb + 1) * L],
                    start=False, stop=(cb == CB - 1)),
                   waits=(cw if ji == 0 else []),
                   incs=([(st, 1)] if cb == CB - 1 else []))
                if cb == CB - 1:
                    state[("st_oj", j)] = st.n
        slot_free_at[osl] = st.n

        def xres_load(ji):
            j = jorder[ji]
            par = ji % 4
            xw = [WV(sv, state["xrso"][par])]
            if ti == 0 and ji < 2:
                xw.append(WV(s_pw, prep.get(("dn", "wdone"), 0)))
            op("sync", lambda e, j=j, par=par, ti=ti: e.dma_start(
                out=xres[par], in_=XFP4[:, ti, j, :]),
               waits=xw, incs=[(s_xr[par], 16)])
            state[("xr", j)] = s_xr[par].n
        for ji in range(4):
            xres_load(ji)
        for ji, j in enumerate(jorder):
            par = ji % 4
            ob = ji % 4
            ew = [WV(st, state[("st_oj", j)]),
                  WV(s_xr[par], state[("xr", j)]),
                  WV(s_o[ob], state["oso"][ob])]
            if ti == 0 and ji < 3:
                ew += [WV(ss, prep.get(("dn", "slast"), 0)),
                       WV(sv, prep.get(("dn", "vlast"), 0))]
            op("vector", lambda e, j=j, par=par, ob=ob: e.tensor_add(
                out=out_sb[ob], in0=pb[j], in1=xres[par]),
               waits=ew, incs=[(sv, 1)])
            sv_ev = sv.n
            state["xrso"][par] = sv_ev
            op("sync", lambda e, j=j, ob=ob, t0=t0: e.dma_start(
                out=OUT[j * P:(j + 1) * P, t0:t0 + L], in_=out_sb[ob]),
               waits=[WV(sv, sv_ev)], incs=[(s_o[ob], 16)])
            state["oso"][ob] = s_o[ob].n
            bank_free[j] = (sv_ev, bank_free[j][1])
            if ji + 4 < CB:
                xres_load(ji + 4)

    for s4 in s_o:
        op("sync", lambda e, s4=s4: e.wait_ge(s4.h, s4.n))

    block = ctx.enter_context(nc.Block())

    def mk_body(eng_name):
        def body(eng):
            for waits, incs, fn in ops[eng_name]:
                for s, v in waits:
                    eng.wait_ge(s.h, v)
                instr = fn(eng)
                for s, k in incs:
                    instr.then_inc(s.h, k)
        return body

    block.sync(mk_body("sync"))
    block.gpsimd(mk_body("gpsimd"))
    block.vector(mk_body("vector"))
    block.scalar(mk_body("scalar"))
    block.tensor(mk_body("tensor"))
    ctx.close()
    return nc


def _host_stage(inputs, T, L=512):
    x = np.asarray(inputs["L0_attn_x"], np.float32)
    B = x.shape[0]
    v8 = lambda a: np.ascontiguousarray(
        np.asarray(a, np.float32).reshape(CB, P).T)

    def pmaj(w):
        wt = np.asarray(w, np.float32).T
        cin, co = wt.shape
        return np.ascontiguousarray(
            wt.reshape(cin // P, P, co).transpose(1, 0, 2).reshape(P, -1))

    def pmaj_up(w):
        wt = np.asarray(w, np.float32).T
        return np.ascontiguousarray(
            wt.reshape(CB, P, NQ, FQ).transpose(1, 2, 0, 3).reshape(P, -1))

    common = dict(
        WKT=pmaj(inputs["L0_attn_w_k"]),
        WVT=pmaj(inputs["L0_attn_w_v"]),
        WRT=pmaj(inputs["L0_attn_w_r"]),
        WOT=pmaj(inputs["L0_attn_w_o"]),
        WUT=pmaj_up(inputs["L0_ffn_w_up"]),
        WDT=pmaj(inputs["L0_ffn_w_down"]),
        TD8=v8(inputs["L0_attn_time_decay"]),
        TF8=v8(inputs["L0_attn_time_first"]),
        MK8=v8(inputs["L0_attn_time_mix_key"]),
        MV8=v8(inputs["L0_attn_time_mix_value"]),
        MR8=v8(inputs["L0_attn_time_mix_receptance"]),
    )
    in_maps = []
    for b in range(B):
        m = dict(common)
        xc = np.ascontiguousarray(x[b, :T].T)
        m["XF"] = np.ascontiguousarray(
            xc.reshape(CB, P, T // L, L).transpose(1, 2, 0, 3).reshape(P, -1))
        in_maps.append(m)
    return in_maps


def kernel(**inputs):
    T = 4096
    in_maps = _host_stage(inputs, T)
    nc = build_program(T=T, L=512)
    res = run_bass_kernel_spmd(nc, in_maps, core_ids=list(range(len(in_maps))))
    outs = [np.ascontiguousarray(np.asarray(r["OUT"]).T) for r in res.results]
    return np.stack(outs, axis=0)
